# version 12
# baseline (speedup 1.0000x reference)
import numpy as np
import ml_dtypes
import contextlib
import concourse.bass as bass
import concourse.mybir as mybir
from concourse.bass_utils import run_bass_kernel_spmd

F32 = mybir.dt.float32
BF16 = mybir.dt.bfloat16
AF = mybir.ActivationFunctionType
ALU = mybir.AluOpType
AX = mybir.AxisListType

D = 1024
NCH = 16
NSLOT = 4
LTRIP = (15, 11, 7, 3)
POS_OWN = (15, 11, 7, 3)
SLOT_OFF = (0, 15, 26, 33)
NITER = 36
NEG_BIG = -30000.0
EPS = 1e-5
NBLK = NCH
PORDER = [15, 11, 7, 3] + [p for p in range(16) if p not in (15, 11, 7, 3)]
POS2SLOT = {POS_OWN[s]: s for s in range(NSLOT)}

ENGS = ("sync", "gpsimd", "tensor", "vector", "scalar")
SEMS = ("lda", "ldf", "lds", "ldb", "ldw", "ldg", "ldg2", "stq", "stq2",
        "ldh0", "ldh1", "sst0", "sst1", "sxn2", "lck0", "lck1", "lck1g",
        "lck1s", "lcx0", "lcx1", "lxn0", "lxn1", "lxf0", "lxf1", "lxf2",
        "lxf3", "sy0", "sy1", "pe", "act", "dve")


def chunk_of(lane, s):
    return 4 * (3 - s) + ((lane + s) % 4)


def perm_of(lane):
    own = {POS_OWN[s]: chunk_of(lane, s) for s in range(NSLOT)}
    rest = sorted(set(range(16)) - set(own.values()))
    perm = []
    ri = 0
    for p in range(16):
        if p in own:
            perm.append(own[p])
        else:
            perm.append(rest[ri])
            ri += 1
    for s in range(NSLOT):
        c = chunk_of(lane, s)
        assert set(range(c)) <= set(perm[:LTRIP[s]]), (lane, s, perm)
    return perm


class Plan:
    def __init__(self, cnt):
        self.ops = {e: [] for e in ENGS}
        self.c = cnt

    def val(self, s):
        return self.c.get(s, 0)

    def add(self, eng, fn, waits=(), inc=None, by=1):
        v = None
        if inc is not None:
            v = self.c.get(inc, 0) + by
            self.c[inc] = v
        waits = tuple((s, w) for (s, w) in waits if w and w > 0)

        def run(E, S, fn=fn, waits=waits, inc=inc, by=by):
            for s, w in waits:
                E.wait_ge(S[s], w)
            ins = fn(E, S)
            if inc is not None:
                assert ins is not None
                ins.then_inc(S[inc], by)
        self.ops[eng].append(run)
        return v


def build_nc():
    nc = bass.Bass("TRN2", target_bir_lowering=False, detect_race_conditions=False)

    x_all = nc.dram_tensor("x_all", [NCH * 1024, D], F32, kind="ExternalInput").ap()
    x_own = nc.dram_tensor("x_own", [NSLOT, 1024, D], F32, kind="ExternalInput").ap()
    WTh = nc.dram_tensor("WTh", [D, 2 * D], F32, kind="ExternalInput").ap()
    wfr = nc.dram_tensor("wfr", [D, D], F32, kind="ExternalInput").ap()
    wfi = nc.dram_tensor("wfi", [D, D], F32, kind="ExternalInput").ap()
    bias_h = nc.dram_tensor("bias_h", [D], F32, kind="ExternalInput").ap()
    Fr_h = nc.dram_tensor("Fr_h", [D, D], BF16, kind="ExternalInput").ap()
    Fi_h = nc.dram_tensor("Fi_h", [D, D], BF16, kind="ExternalInput").ap()
    Gr_h = nc.dram_tensor("Gr_h", [D, D], BF16, kind="ExternalInput").ap()
    Gin_h = nc.dram_tensor("Gin_h", [D, D], BF16, kind="ExternalInput").ap()
    ident_h = nc.dram_tensor("ident_h", [128, 128], BF16, kind="ExternalInput").ap()
    trimask_h = nc.dram_tensor("trimask_h", [128, 128], BF16, kind="ExternalInput").ap()
    btab_h = nc.dram_tensor("btab_h", [128, NITER], F32, kind="ExternalInput").ap()
    y_own = nc.dram_tensor("y_own", [NSLOT, 1024, D], F32, kind="ExternalOutput").ap()

    KT_scr = nc.dram_tensor("KT_scr", [D, NBLK * 1024], BF16, kind="Internal").ap()
    V_scr = nc.dram_tensor("V_scr", [NBLK * 1024, D], BF16, kind="Internal").ap()
    xnT_scr = nc.dram_tensor("xnT_scr", [NSLOT, 8, 128, 1024], BF16, kind="Internal").ap()
    xn2_scr = nc.dram_tensor("xn2_scr", [NSLOT, 8, 128, D], BF16, kind="Internal").ap()

    xall_t = x_all.rearrange("(n p) d -> p n d", p=128)
    xown_t = x_own.rearrange("s (n p) d -> p s n d", p=128)
    ktscr_t = KT_scr.rearrange("(a p) t -> p a t", p=128)
    vscr_t = V_scr.rearrange("(n p) d -> p n d", p=128)
    xnscr_t = xnT_scr.rearrange("s c p t -> p s c t")
    xn2scr_t = xn2_scr.rearrange("s q p d -> p s q d")
    yown_t = y_own.rearrange("s (n p) d -> p s n d", p=128)

    cnt = {}
    with contextlib.ExitStack() as st:
        S = {}
        for name in SEMS:
            S[name] = st.enter_context(nc.semaphore(name))

        def dispatch(plan):
            with nc.Block() as blk:
                for e in ENGS:
                    ops = plan.ops[e]
                    if not ops:
                        continue

                    def body(E, ops=ops):
                        for f in ops:
                            f(E, S)
                    getattr(blk, e)(body)

        M_sb = st.enter_context(nc.sbuf_tensor("M_sb", [128, 8, D], BF16))
        ones1f = st.enter_context(nc.sbuf_tensor("ones1f", [1, 128], F32))
        biasrow = st.enter_context(nc.sbuf_tensor("biasrow", [1, D], F32))
        ident = st.enter_context(nc.sbuf_tensor("ident", [128, 128], BF16))
        trimask = st.enter_context(nc.sbuf_tensor("trimask", [128, 128], BF16))
        btab = st.enter_context(nc.sbuf_tensor("btab", [128, NITER], F32))
        cstk = st.enter_context(nc.sbuf_tensor("cstk", [128, 4], F32))

        with contextlib.ExitStack() as stAB:
            WT_sb = stAB.enter_context(nc.sbuf_tensor("WT_sb", [128, 8, 2 * D], BF16))
            xstB = stAB.enter_context(nc.sbuf_tensor("xstB", [128, 2, 4, D], F32))
            xn_bf = stAB.enter_context(nc.sbuf_tensor("xn_bf", [128, 8, D], BF16))
            bns = stAB.enter_context(nc.sbuf_tensor("bns", [128, 8, 12], F32))
            mv2 = stAB.enter_context(nc.sbuf_tensor("mv2", [128, 8, 2], F32))
            sv = stAB.enter_context(nc.sbuf_tensor("sv", [128, 8], F32))
            r8 = stAB.enter_context(nc.sbuf_tensor("r8", [128, 8], F32))

            v_xn = {}
            v_evxnT = {}
            v_store = {}
            v_xn2st = {}
            vevK = {}
            vevV = {}
            v_half = {}
            v_trB = {}

            def emit_ln_stats(plan, idx):
                vst2 = {}
                for q in range(8):
                    hid = 2 * idx + q // 4
                    hp = hid % 2
                    xsrc = xstB[:, hp, q % 4, :]
                    w = [("ldh%d" % hp, v_half[hid])] if q % 4 == 0 else []
                    plan.add("vector", lambda E, S, xsrc=xsrc, q=q: E.bn_stats(bns[:, q, 0:6], xsrc[:, 0:512]), waits=w)
                    vst2[q] = plan.add("vector", lambda E, S, xsrc=xsrc, q=q: E.bn_stats(bns[:, q, 6:12], xsrc[:, 512:1024]), inc="dve")
                va = None
                for q in range(8):
                    va = plan.add("vector", lambda E, S, q=q: E.bn_aggr(mv2[:, q, :], bns[:, q, :]),
                                  waits=[("dve", vst2[q])], inc="dve")
                return va

            def emit_ln_fin(plan, va):
                vsq = plan.add("scalar", lambda E, S: E.activation(sv[:, 0:8], mv2[:, :, 1:2].rearrange("p q o -> p (q o)"),
                                                                  AF.Sqrt, bias=cstk[:, 0:1]),
                               waits=[("dve", va)], inc="act")
                vrc = plan.add("vector", lambda E, S: E.reciprocal(r8[:, 0:8], sv[:, 0:8]), waits=[("act", vsq)], inc="dve")
                return vrc

            def emit_ln_ts(plan, idx, vrc):
                for q in range(8):
                    ti = idx * 8 + q
                    hid = 2 * idx + q // 4
                    hp = hid % 2
                    xsrc = xstB[:, hp, q % 4, :]
                    w = [("dve", vrc), ("pe", v_trB.get(ti - 8, 0)), ("sxn2", v_xn2st.get(idx - 1, 0))]
                    v_xn[ti] = plan.add("vector", lambda E, S, xsrc=xsrc, q=q: E.tensor_scalar(
                        xn_bf[:, q, :], xsrc, mv2[:, q, 0:1], r8[:, q:q + 1], ALU.subtract, ALU.mult),
                        waits=w, inc="dve")

            pA = Plan(cnt)
            with contextlib.ExitStack() as stA:
                ArT = stA.enter_context(nc.sbuf_tensor("ArT", [128, 8, D], BF16))
                AiT = stA.enter_context(nc.sbuf_tensor("AiT", [128, 8, D], BF16))
                psA0 = stA.enter_context(nc.psum_tensor("psA0", [128, D], F32))
                psA1 = stA.enter_context(nc.psum_tensor("psA1", [128, D], F32))
                psA = (psA0, psA1)

                pos0 = PORDER[0]
                for h in range(2):
                    src_ = xall_t[:, pos0 * 8 + h * 4:pos0 * 8 + h * 4 + 4, :]
                    v_half[h] = pA.add("scalar", lambda E, S, src_=src_, h=h: E.dma_start(out=xstB[:, h, :, :], in_=src_),
                                       inc="ldh%d" % h, by=16)

                vbr = pA.add("sync", lambda E, S: E.dma_start(out=biasrow[:], in_=bias_h[None, :]), inc="ldb", by=16)
                vones = pA.add("vector", lambda E, S: E.memset(ones1f[:], 1.0), inc="dve")
                pA.add("vector", lambda E, S: E.memset(cstk[:, 0:1], EPS))
                pA.add("vector", lambda E, S: E.memset(cstk[:, 1:2], -0.5))
                pA.add("vector", lambda E, S: E.memset(cstk[:, 2:3], -1.0), inc="dve")
                va0 = emit_ln_stats(pA, 0)

                with contextlib.ExitStack() as stA1:
                    Fr_sb = stA1.enter_context(nc.sbuf_tensor("Fr_sb", [128, 8, D], BF16))
                    Fi_sb = stA1.enter_context(nc.sbuf_tensor("Fi_sb", [128, 8, D], BF16))
                    wr_bf = stA1.enter_context(nc.sbuf_tensor("wr_bf", [128, 8, D], BF16))
                    wi_bf = stA1.enter_context(nc.sbuf_tensor("wi_bf", [128, 8, D], BF16))

                    vFr = pA.add("sync", lambda E, S: E.dma_start(out=Fr_sb[:], in_=Fr_h.rearrange("(c p) t -> p c t", p=128)), inc="ldf", by=16)
                    pA.add("sync", lambda E, S: E.dma_start(out=ident[:], in_=ident_h), inc="lda", by=16)
                    pA.add("sync", lambda E, S: E.dma_start(out=trimask[:], in_=trimask_h), inc="lda", by=16)
                    vlda = pA.add("sync", lambda E, S: E.dma_start(out=btab[:], in_=btab_h), inc="lda", by=16)
                    vFi = pA.add("scalar", lambda E, S: E.dma_start(out=Fi_sb[:], in_=Fi_h.rearrange("(c p) t -> p c t", p=128)), inc="lds", by=16)
                    vwi = pA.add("gpsimd", lambda E, S: E.dma_start(out=wi_bf[:], in_=wfi.rearrange("(c p) o -> p c o", p=128)), inc="stq", by=16)
                    vwr = pA.add("gpsimd", lambda E, S: E.dma_start(out=wr_bf[:], in_=wfr.rearrange("(c p) o -> p c o", p=128)), inc="stq2", by=16)
                    vWT = pA.add("gpsimd", lambda E, S: E.dma_start(out=WT_sb[:], in_=WTh.rearrange("(c p) f -> p c f", p=128)),
                                 inc="ldw", by=16)

                    evA = []
                    vneg = None
                    vmmlast = None
                    for grp in range(2):
                        if grp == 0:
                            dstT, l0, l1 = AiT, wi_bf, wr_bf
                            w_a = [("stq", vwi), ("ldf", vFr)]
                            w_b = [("stq2", vwr), ("lds", vFi)]
                        else:
                            dstT, l0, l1 = ArT, wr_bf, wi_bf
                            w_a = [("stq2", vwr), ("ldf", vFr)]
                            w_b = [("dve", vneg), ("lds", vFi)]
                        for ot in range(8):
                            idx = grp * 8 + ot
                            ps = psA[idx % 2]
                            wa = list(w_a)
                            if idx >= 2:
                                wa.append(("act", evA[idx - 2]))

                            def mm_p1(E, S, ps=ps, l0=l0, ot=ot):
                                last = None
                                for h in range(2):
                                    for d in range(8):
                                        last = E.matmul(
                                            ps[:, h * 512:(h + 1) * 512],
                                            l0[:, d, ot * 128:(ot + 1) * 128],
                                            Fr_sb[:, d, h * 512:(h + 1) * 512],
                                            start=(d == 0), stop=False)
                                return last
                            pA.add("tensor", mm_p1, waits=wa, inc="pe")

                            def mm_p2(E, S, ps=ps, l1=l1, ot=ot):
                                last = None
                                for h in range(2):
                                    for d in range(8):
                                        last = E.matmul(
                                            ps[:, h * 512:(h + 1) * 512],
                                            l1[:, d, ot * 128:(ot + 1) * 128],
                                            Fi_sb[:, d, h * 512:(h + 1) * 512],
                                            start=False, stop=(d == 7))
                                return last
                            vmmlast = pA.add("tensor", mm_p2, waits=list(w_b), inc="pe")
                            evA.append(pA.add("scalar", lambda E, S, ps=ps, dstT=dstT, ot=ot: E.copy(dstT[:, ot, :], ps[:]),
                                              waits=[("pe", vmmlast)], inc="act"))
                        if grp == 0:
                            vneg = pA.add("vector", lambda E, S: E.tensor_scalar_mul(wi_bf[:], wi_bf[:], -1.0),
                                          waits=[("pe", vmmlast)], inc="dve")
                            vrc0 = emit_ln_fin(pA, va0)
                    vA_ev_done = evA[-1]
                    emit_ln_ts(pA, 0, vrc0)

                with contextlib.ExitStack() as stA2:
                    Gr_sb = stA2.enter_context(nc.sbuf_tensor("Gr_sb", [128, 8, D], BF16))
                    Gin_sb = stA2.enter_context(nc.sbuf_tensor("Gin_sb", [128, 8, D], BF16))
                    vGr = pA.add("sync", lambda E, S: E.dma_start(out=Gr_sb[:], in_=Gr_h.rearrange("(c p) t -> p c t", p=128)),
                                 waits=[("pe", vmmlast)], inc="ldg", by=16)
                    vGin = pA.add("sync", lambda E, S: E.dma_start(out=Gin_sb[:], in_=Gin_h.rearrange("(c p) t -> p c t", p=128)),
                                  inc="ldg2", by=16)

                    evM = []
                    for tt in range(8):
                        ps = psA[tt % 2]
                        wa = [("ldg", vGr), ("act", vA_ev_done)]
                        if tt >= 2:
                            wa.append(("act", evM[tt - 2]))

                        def mm_a(E, S, ps=ps, tt=tt):
                            last = None
                            for h in range(2):
                                for k in range(8):
                                    last = E.matmul(ps[:, h * 512:(h + 1) * 512],
                                                    ArT[:, k, tt * 128:(tt + 1) * 128],
                                                    Gr_sb[:, k, h * 512:(h + 1) * 512],
                                                    start=(k == 0), stop=False)
                            return last
                        pA.add("tensor", mm_a, waits=wa, inc="pe")

                        def mm_b(E, S, ps=ps, tt=tt):
                            last = None
                            for h in range(2):
                                for k in range(8):
                                    last = E.matmul(ps[:, h * 512:(h + 1) * 512],
                                                    AiT[:, k, tt * 128:(tt + 1) * 128],
                                                    Gin_sb[:, k, h * 512:(h + 1) * 512],
                                                    start=False, stop=(k == 7))
                            return last
                        vmm = pA.add("tensor", mm_b, waits=[("ldg2", vGin)], inc="pe")
                        evM.append(pA.add("scalar", lambda E, S, ps=ps, tt=tt: E.copy(M_sb[:, tt, :], ps[:]),
                                          waits=[("pe", vmm)], inc="act"))
                vA_act = pA.val("act")
                vA_pe = pA.val("pe")
                vA_dve = pA.val("dve")
                dispatch(pA)

            pB = Plan(cnt)
            barrier = (("act", vA_act), ("pe", vA_pe), ("dve", vA_dve))
            with contextlib.ExitStack() as stB:
                xnT_B = stB.enter_context(nc.sbuf_tensor("xnT_B", [128, 2, 8, D], BF16))
                KT_blk = stB.enter_context(nc.sbuf_tensor("KT_blk", [128, 2, 8, D], BF16))
                V_blk = stB.enter_context(nc.sbuf_tensor("V_blk", [128, 2, 8, D], BF16))
                psK0 = stB.enter_context(nc.psum_tensor("psK0", [128, D], F32))
                psK1 = stB.enter_context(nc.psum_tensor("psK1", [128, D], F32))
                psV0 = stB.enter_context(nc.psum_tensor("psV0", [128, D], F32))
                psV1 = stB.enter_context(nc.psum_tensor("psV1", [128, D], F32))
                psK = (psK0, psK1)
                psV = (psV0, psV1)

                def emit_halves(idx):
                    pos = PORDER[idx]
                    for h in range(2):
                        hid = 2 * idx + h
                        hp = hid % 2
                        src_ = xall_t[:, pos * 8 + h * 4:pos * 8 + h * 4 + 4, :]
                        w = [("dve", v_xn.get((hid - 2) * 4 + 3, 0))]
                        v_half[hid] = pB.add("sync", lambda E, S, src_=src_, hp=hp: E.dma_start(out=xstB[:, hp, :, :], in_=src_),
                                             waits=w, inc="ldh%d" % hp, by=16)

                def emit_xn2(idx):
                    s_ = POS2SLOT[PORDER[idx]]
                    v_xn2st[idx] = pB.add("gpsimd", lambda E, S, s_=s_: E.dma_start(out=xn2scr_t[:, s_, :, :], in_=xn_bf[:]),
                                          waits=[("dve", v_xn[idx * 8 + 7])], inc="sxn2", by=16)

                def emit_tr(idx):
                    jp = idx % 2
                    regs = (psK[0], psK[1], psV[0], psV[1])
                    for q in range(8):
                        ti = idx * 8 + q
                        r4 = q % 4
                        wtr = [("dve", v_xn[ti]), ("lda", vlda)]
                        if r4 < 2:
                            wtr.append(("act", vevK.get((idx - 1, 6 + r4), 0)))
                        else:
                            wtr.append(("act", vevV.get((idx - 1, 4 + r4), 0)))
                        pv = v_evxnT.get(ti - 4)
                        if pv is not None:
                            wtr.append((pv[0], pv[1]))
                        if ti == 0:
                            wtr += list(barrier)

                        def trB(E, S, q=q, r4=r4, regs=regs):
                            last = None
                            for d in range(8):
                                last = E.matmul(regs[r4][:, d * 128:(d + 1) * 128], xn_bf[:, q, d * 128:(d + 1) * 128],
                                                ident[:], start=True, stop=True)
                            return last
                        vtr = pB.add("tensor", trB, waits=wtr, inc="pe")
                        v_trB[ti] = vtr
                        wev = [("pe", vtr)]
                        if q == 0:
                            wev.append(("sst%d" % jp, v_store.get(idx - 2, 0)))

                        def evxnT_v(E, S, r4=r4, jp=jp, q=q, regs=regs):
                            return E.tensor_copy(xnT_B[:, jp, :, q * 128:(q + 1) * 128], regs[r4].rearrange("p (c t) -> p c t", c=8))

                        def evxnT_s(E, S, r4=r4, jp=jp, q=q, regs=regs):
                            return E.copy(xnT_B[:, jp, :, q * 128:(q + 1) * 128], regs[r4].rearrange("p (c t) -> p c t", c=8))
                        if q % 2 == 0:
                            v_evxnT[ti] = ("dve", pB.add("vector", evxnT_v, waits=wev, inc="dve"))
                        else:
                            v_evxnT[ti] = ("act", pB.add("scalar", evxnT_s, waits=wev, inc="act"))

                def _evw(idx, qmax):
                    wd = wa = 0
                    for q in range(qmax + 1):
                        e, v = v_evxnT[idx * 8 + q]
                        if e == "dve":
                            wd = max(wd, v)
                        else:
                            wa = max(wa, v)
                    return [("dve", wd), ("act", wa)]

                def emit_kv(idx, f0, f1):
                    jp = idx % 2
                    for ftile in range(f0, f1):
                        for kind in range(2):
                            kp = ftile % 2
                            for h in range(2):
                                wmm = _evw(idx, 3 if h == 0 else 7)
                                if idx == 0 and ftile == 0 and h == 0:
                                    wmm.append(("ldw", vWT))
                                if h == 0:
                                    if kind == 0:
                                        if ftile >= 2:
                                            wmm.append(("act", vevK.get((idx, ftile - 2), 0)))
                                        else:
                                            e, v = v_evxnT[idx * 8 + 4 + kp]
                                            wmm.append((e, v))
                                    else:
                                        if ftile >= 2:
                                            wmm.append(("act", vevV.get((idx, ftile - 2), 0)))
                                        else:
                                            e, v = v_evxnT[idx * 8 + 6 + kp]
                                            wmm.append((e, v))
                                if kind == 0:
                                    def mmK(E, S, jp=jp, ft=ftile, kp=kp, h=h):
                                        last = None
                                        for d in range(8):
                                            last = E.matmul(psK[kp][:, h * 512:(h + 1) * 512],
                                                            WT_sb[:, d, ft * 128:(ft + 1) * 128],
                                                            xnT_B[:, jp, d, h * 512:(h + 1) * 512],
                                                            start=(d == 0), stop=(d == 7))
                                        return last
                                    vmm = pB.add("tensor", mmK, waits=wmm, inc="pe")
                                else:
                                    def mmV(E, S, jp=jp, tt=ftile, kp=kp, h=h):
                                        last = None
                                        for d in range(8):
                                            last = E.matmul(psV[kp][:, h * 512:(h + 1) * 512],
                                                            xnT_B[:, jp, d, tt * 128:(tt + 1) * 128],
                                                            WT_sb[:, d, D + h * 512:D + (h + 1) * 512],
                                                            start=(d == 0), stop=(d == 7))
                                        return last
                                    vmm = pB.add("tensor", mmV, waits=wmm, inc="pe")
                            wev = [("pe", vmm)]
                            if ftile < 2:
                                wev.append(("sst%d" % jp, v_store.get(idx - 2, 0)))
                            if kind == 0:
                                vevK[(idx, ftile)] = pB.add("scalar", lambda E, S, jp=jp, ft=ftile, kp=kp: E.copy(KT_blk[:, jp, ft, :], psK[kp][:]),
                                                            waits=wev, inc="act")
                            else:
                                vevV[(idx, ftile)] = pB.add("scalar", lambda E, S, jp=jp, tt=ftile, kp=kp: E.copy(V_blk[:, jp, tt, :], psV[kp][:]),
                                                            waits=wev, inc="act")

                def emit_stores(idx):
                    jp = idx % 2
                    pos = PORDER[idx]
                    sem = "sst%d" % jp
                    pB.add("gpsimd", lambda E, S, jp=jp, pos=pos: E.dma_start(out=ktscr_t[:, :, pos * 1024:(pos + 1) * 1024], in_=KT_blk[:, jp, :, :]),
                           waits=[("act", vevK[(idx, 7)])], inc=sem, by=16)
                    vst = pB.add("gpsimd", lambda E, S, jp=jp, pos=pos: E.dma_start(out=vscr_t[:, pos * 8:(pos + 1) * 8, :], in_=V_blk[:, jp, :, :]),
                                 waits=[("act", vevV[(idx, 7)])], inc=sem, by=16)
                    if pos in POS2SLOT:
                        s_ = POS2SLOT[pos]
                        e, v = v_evxnT[idx * 8 + 7]
                        vst = pB.add("gpsimd", lambda E, S, jp=jp, s_=s_: E.dma_start(out=xnscr_t[:, s_, :, :], in_=xnT_B[:, jp, :, :]),
                                     waits=[(e, v), ("dve", v_evxnT[idx * 8 + 6][1])], inc=sem, by=16)
                    v_store[idx] = vst

                for idx in range(NBLK):
                    if PORDER[idx] in POS2SLOT:
                        emit_xn2(idx)
                    emit_tr(idx)
                    emit_kv(idx, 0, 4)
                    if idx + 1 < NBLK:
                        emit_halves(idx + 1)
                        va = emit_ln_stats(pB, idx + 1)
                        vrc = emit_ln_fin(pB, va)
                    emit_kv(idx, 4, 8)
                    if idx + 1 < NBLK:
                        emit_ln_ts(pB, idx + 1, vrc)
                    emit_stores(idx)

                vB_act = pB.val("act")
                vB_pe = pB.val("pe")
                vB_dve = pB.val("dve")
                vB_sst0 = pB.val("sst0")
                vB_sst1 = pB.val("sst1")
                vB_sxn2 = pB.val("sxn2")
                dispatch(pB)

        pC = Plan(cnt)
        with contextlib.ExitStack() as stC:
            KTs = stC.enter_context(nc.sbuf_tensor("KTs", [128, 2, 8, D], BF16))
            Vs = stC.enter_context(nc.sbuf_tensor("Vs", [128, 2, 8, D], BF16))
            acc = stC.enter_context(nc.sbuf_tensor("acc", [128, 8, D], F32))
            P_sb = stC.enter_context(nc.sbuf_tensor("P_sb", [128, 2, D], BF16))
            PT_sb = stC.enter_context(nc.sbuf_tensor("PT_sb", [128, D], BF16))
            xnT_own = stC.enter_context(nc.sbuf_tensor("xnT_own", [128, 2, 8, D], BF16))
            xn_ld = stC.enter_context(nc.sbuf_tensor("xn_ld", [128, 2, D], BF16))
            xstC = stC.enter_context(nc.sbuf_tensor("xstC", [128, 4, D], F32))
            s_sb = stC.enter_context(nc.sbuf_tensor("s_sb", [128, 2, D], F32))
            yb_sb = stC.enter_context(nc.sbuf_tensor("yb_sb", [128, 2, D], F32))
            gsil = stC.enter_context(nc.sbuf_tensor("gsil", [128, 2, D], F32))
            negm = stC.enter_context(nc.sbuf_tensor("negm", [128, 2, 8], F32))
            lsm = stC.enter_context(nc.sbuf_tensor("lsm", [128, 2, 8], F32))
            bP = stC.enter_context(nc.sbuf_tensor("bP", [128, 2, 8], F32))
            rr = stC.enter_context(nc.sbuf_tensor("rr", [128, 2, 8], F32))
            tmpm = stC.enter_context(nc.sbuf_tensor("tmpm", [128, 8], F32))
            lpart = stC.enter_context(nc.sbuf_tensor("lpart", [128, 8], F32))
            invl = stC.enter_context(nc.sbuf_tensor("invl", [128, 8], F32))
            psS0 = stC.enter_context(nc.psum_tensor("psS0", [128, D], F32))
            psS1 = stC.enter_context(nc.psum_tensor("psS1", [128, D], F32))
            psT = stC.enter_context(nc.psum_tensor("psT", [128, D], F32))
            psVC = stC.enter_context(nc.psum_tensor("psVC", [128, D], F32))
            psS = (psS0, psS1)

            barrierB = (("act", vB_act), ("pe", vB_pe), ("dve", vB_dve))
            stqB = (("sst0", vB_sst0), ("sst1", vB_sst1), ("sxn2", vB_sxn2))
            last_rd_buf = {0: 0, 1: 0}
            fin_pe_bypar = {0: 0, 1: 0}
            prev_ex_bybank = {0: 0, 1: 0}
            prev_tr_bypar = {0: 0, 1: 0}
            prev_evt_last = 0
            prev_psVC_free = None
            prev_fin_dma = {0: 0, 1: 0}
            prev_out4 = {0: 0, 1: 0, 2: 0, 3: 0}
            prev_Sfree_bypar = {0: 0, 1: 0}
            prev_pool_bypar = {0: 0, 1: 0}

            def emit_prep(s):
                sp = s % 2
                dblk = POS_OWN[s]
                xq = "scalar" if s == 0 else "sync"
                kq, ksem = ("scalar", "lck1s") if s == 0 else ("gpsimd", "lck1g")
                wp = [("pe", fin_pe_bypar[sp])]
                if s == 0:
                    wp += list(barrierB) + list(stqB)
                v_xnl = pC.add(xq, lambda E, S, s=s, sp=sp: E.dma_start(out=xnT_own[:, sp, :, :], in_=xnscr_t[:, s, :, :]),
                               waits=wp, inc="lcx%d" % sp, by=16)
                wk = [("pe", last_rd_buf[0])]
                if s == 0:
                    wk += list(barrierB) + list(stqB)
                pC.add("sync", lambda E, S, dblk=dblk: E.dma_start(out=KTs[:, 0, :, :], in_=ktscr_t[:, :, dblk * 1024:(dblk + 1) * 1024]),
                       waits=wk, inc="lck0", by=16)
                v_vl = pC.add("sync", lambda E, S, dblk=dblk: E.dma_start(out=Vs[:, 0, :, :], in_=vscr_t[:, dblk * 8:(dblk + 1) * 8, :]),
                              inc="lck0", by=16)
                wk1 = [("pe", last_rd_buf[1])]
                if s == 0:
                    wk1 += list(barrierB) + list(stqB)
                pC.add(kq, lambda E, S: E.dma_start(out=KTs[:, 1, :, :], in_=ktscr_t[:, :, 0:1024]),
                       waits=wk1, inc=ksem, by=16)
                v_v1 = pC.add(kq, lambda E, S: E.dma_start(out=Vs[:, 1, :, :], in_=vscr_t[:, 0:8, :]),
                              inc=ksem, by=16)
                return {"xnl": ("lcx%d" % sp, v_xnl), "pair": {0: ("lck0", v_vl), 1: (ksem, v_v1)}}

            pending = emit_prep(0)

            for s in range(NSLOT):
                L = LTRIP[s]
                sp = s % 2
                v_xnl = pending["xnl"]
                v_ldpair = dict(pending["pair"])

                def emit_load(j):
                    bp = (j + 1) % 2
                    sem = "lck%d" % bp
                    pC.add("sync", lambda E, S, bp=bp, j=j: E.dma_start(out=KTs[:, bp, :, :], in_=ktscr_t[:, :, j * 1024:(j + 1) * 1024]),
                           waits=[("pe", last_rd_buf[bp])], inc=sem, by=16)
                    vv = pC.add("sync", lambda E, S, bp=bp, j=j: E.dma_start(out=Vs[:, bp, :, :], in_=vscr_t[:, j * 8:(j + 1) * 8, :]),
                               inc=sem, by=16)
                    v_ldpair[j + 1] = (sem, vv)

                steps = [("diag", 0, t) for t in range(8)]
                for j in range(L):
                    steps += [("stream", j + 1, t) for t in range(8)]
                K = len(steps)

                vSC = {}
                vTR = {}
                vPV = {}
                vST = {}
                vUP = {}
                vEX = {}
                vR = {}
                vEVT = {}

                def bufp_of(i):
                    return 0 if i == 0 else i % 2

                def emit_SC(k):
                    kind, i, t = steps[k]
                    bp = bufp_of(i)
                    ps = psS[k % 2]
                    w = []
                    if k >= 2:
                        w.append(("act", vEX[k - 2]))
                    else:
                        w.append(("act", prev_ex_bybank[k % 2]))
                        w.append(v_xnl)
                        w.append(v_ldpair[0])
                        w.append(("lda", vlda))
                    if t == 0 and i > 0:
                        w.append(v_ldpair[i])
                    if kind == "diag":
                        ncols = (t + 1) * 128

                        def fn(E, S, t=t, bp=bp, ncols=ncols, sp=sp, ps=ps):
                            last = None
                            tri_lo = ncols - 128
                            for h in range(2):
                                lo = h * 512
                                hi = min(ncols, (h + 1) * 512)
                                if hi <= lo:
                                    break
                                has_tri = (tri_lo >= lo) and (ncols <= (h + 1) * 512)
                                for d in range(8):
                                    last = E.matmul(ps[:, lo:hi], xnT_own[:, sp, d, t * 128:(t + 1) * 128],
                                                    KTs[:, bp, d, lo:hi], start=(d == 0),
                                                    stop=(d == 7 and not has_tri), skip_group_check=True)
                                if has_tri:
                                    last = E.matmul(ps[:, tri_lo:ncols], ident[:], trimask[:],
                                                    start=False, stop=True, skip_group_check=True)
                            return last
                    else:
                        def fn(E, S, t=t, bp=bp, sp=sp, ps=ps):
                            last = None
                            for h in range(2):
                                for d in range(8):
                                    last = E.matmul(ps[:, h * 512:(h + 1) * 512], xnT_own[:, sp, d, t * 128:(t + 1) * 128],
                                                    KTs[:, bp, d, h * 512:(h + 1) * 512], start=(d == 0), stop=(d == 7))
                            return last
                    vSC[k] = pC.add("tensor", fn, waits=w, inc="pe")

                def emit_stats(k):
                    kind, i, t = steps[k]
                    p, q = i % 2, 1 - i % 2
                    ps = psS[k % 2]
                    w = [("pe", vSC[k])]
                    if kind == "diag":
                        ncols = (t + 1) * 128
                        vST[k] = pC.add("vector", lambda E, S, p=p, t=t, ncols=ncols, ps=ps: E.reduce_max(negm[:, p, t:t + 1], ps[:, 0:ncols], axis=AX.X, negate=True),
                                        waits=w, inc="dve")
                    else:
                        itcol = SLOT_OFF[s] + (i - 1)
                        v1 = pC.add("vector", lambda E, S, t=t, ps=ps: E.reduce_max(tmpm[:, t:t + 1], ps[:], axis=AX.X, negate=True),
                                    waits=w, inc="dve")
                        v2 = pC.add("vector", lambda E, S, p=p, q=q, t=t, itcol=itcol: E.tensor_scalar(negm[:, p, t:t + 1], tmpm[:, t:t + 1], btab[:, itcol:itcol + 1], negm[:, q, t:t + 1], ALU.add, ALU.min),
                                    waits=[("dve", v1)], inc="dve")
                        vST[k] = pC.add("vector", lambda E, S, p=p, t=t, itcol=itcol: E.tensor_scalar_sub(bP[:, p, t:t + 1], negm[:, p, t:t + 1], btab[:, itcol:itcol + 1]),
                                        waits=[("dve", v2)], inc="dve")

                def emit_R_EX(k):
                    kind, i, t = steps[k]
                    p, q = i % 2, 1 - i % 2
                    pp = k % 2
                    ps = psS[pp]
                    ncols = (t + 1) * 128 if kind == "diag" else 1024
                    if kind == "stream":
                        vR[k] = pC.add("scalar", lambda E, S, p=p, q=q, t=t: E.activation(rr[:, p, t:t + 1], negm[:, q, t:t + 1], AF.Exp, bias=negm[:, p, t:t + 1], scale=cstk[:, 2:3]),
                                       waits=[("dve", vST[k])], inc="act")
                        bt = bP
                    else:
                        vR[k] = None
                        bt = negm

                    def fn(E, S, bt=bt, p=p, t=t, pp=pp, ncols=ncols, ps=ps):
                        return E.activation(P_sb[:, pp, 0:ncols], ps[:, 0:ncols], AF.Exp,
                                            bias=bt[:, p, t:t + 1], accum_out=lpart[:, t:t + 1])
                    wex = [("dve", vST[k]), ("pe", vTR.get(k - 2, prev_tr_bypar[pp]))]
                    vEX[k] = pC.add("scalar", fn, waits=wex, inc="act")

                def emit_TR(k):
                    kind, i, t = steps[k]
                    nsub = (t + 1) if kind == "diag" else 8
                    pp = k % 2
                    w = [("act", vEX[k])]
                    if k == 0:
                        w.append(("act", prev_evt_last))

                    def fn(E, S, nsub=nsub, pp=pp):
                        last = None
                        for jj in range(nsub):
                            last = E.matmul(psT[:, jj * 128:(jj + 1) * 128], P_sb[:, pp, jj * 128:(jj + 1) * 128],
                                            ident[:], start=True, stop=True)
                        return last
                    vTR[k] = pC.add("tensor", fn, waits=w, inc="pe")

                def emit_EVT(k):
                    kind, i, t = steps[k]
                    ncols = ((t + 1) if kind == "diag" else 8) * 128
                    vEVT[k] = pC.add("scalar", lambda E, S, ncols=ncols: E.copy(PT_sb[:, 0:ncols], psT[:, 0:ncols]),
                                     waits=[("pe", vTR[k])], inc="act")

                def emit_PV(k):
                    kind, i, t = steps[k]
                    nsub = (t + 1) if kind == "diag" else 8
                    bp = bufp_of(i)
                    w = [("act", vEVT[k])]
                    if k > 0:
                        w.append(("dve", vUP[k - 1]))
                    elif prev_psVC_free:
                        w.append(prev_psVC_free)
                    if t == 0 and i > 0:
                        w.append(v_ldpair[i])

                    def fn(E, S, nsub=nsub, bp=bp):
                        last = None
                        for jj in range(nsub):
                            for h in range(2):
                                last = E.matmul(psVC[:, h * 512:(h + 1) * 512], PT_sb[:, jj * 128:(jj + 1) * 128],
                                                Vs[:, bp, jj, h * 512:(h + 1) * 512], start=(jj == 0), stop=(jj == nsub - 1))
                        return last
                    vPV[k] = pC.add("tensor", fn, waits=w, inc="pe")
                    last_rd_buf[bp] = vPV[k]

                def emit_UP(k):
                    kind, i, t = steps[k]
                    p, q = i % 2, 1 - i % 2
                    w = [("pe", vPV[k])]
                    if kind == "diag":
                        pC.add("vector", lambda E, S, p=p, t=t: E.tensor_copy(lsm[:, p, t:t + 1], lpart[:, t:t + 1]), waits=w)
                        vUP[k] = pC.add("vector", lambda E, S, t=t: E.tensor_copy(acc[:, t, :], psVC[:]), inc="dve")
                    else:
                        w.append(("act", vR[k]))
                        pC.add("vector", lambda E, S, p=p, q=q, t=t: E.scalar_tensor_tensor(lsm[:, p, t:t + 1], lsm[:, q, t:t + 1], rr[:, p, t:t + 1], lpart[:, t:t + 1], ALU.mult, ALU.add),
                               waits=w)
                        vUP[k] = pC.add("vector", lambda E, S, p=p, t=t: E.scalar_tensor_tensor(acc[:, t, :], acc[:, t, :], rr[:, p, t:t + 1], psVC[:], ALU.mult, ALU.add),
                                        inc="dve")

                emit_SC(0)
                emit_stats(0)
                emit_R_EX(0)
                emit_SC(1)
                emit_stats(1)
                emit_R_EX(1)
                for k in range(K):
                    kind, i, t = steps[k]
                    emit_TR(k)
                    emit_EVT(k)
                    if k + 2 < K:
                        emit_SC(k + 2)
                        emit_stats(k + 2)
                        emit_R_EX(k + 2)
                    emit_PV(k)
                    emit_UP(k)
                    if t == 7:
                        nx = i + 2
                        if 2 <= nx <= L:
                            emit_load(nx - 1)

                if s + 1 < NSLOT:
                    pending = emit_prep(s + 1)

                lp = L % 2
                vS_ = {}
                vY = {}
                vG = {}
                vMU = {}
                vOUT = {}
                vfdma = {}
                vinvl = {}
                vxnl2 = {}
                vXF = {}
                for t in range(2):
                    vxnl2[t] = pC.add("scalar", lambda E, S, s=s, t=t: E.dma_start(out=xn_ld[:, t % 2, :], in_=xn2scr_t[:, s, t, :]),
                                      waits=[("dve", prev_Sfree_bypar[t % 2]), ("sxn2", vB_sxn2)], inc="lxn%d" % (t % 2), by=16)
                for t in range(4):
                    vXF[t] = pC.add("scalar", lambda E, S, s=s, t=t: E.dma_start(out=xstC[:, t % 4, :], in_=xown_t[:, s, t, :]),
                                    waits=[("dve", prev_out4[t % 4])], inc="lxf%d" % (t % 4), by=16)
                for t in range(8):
                    tp = t % 2
                    t4 = t % 4
                    vinvl[t] = pC.add("vector", lambda E, S, t=t, lp=lp: E.reciprocal(invl[:, t:t + 1], lsm[:, lp, t:t + 1]),
                                      waits=[("dve", vUP[K - 8 + t])], inc="dve")
                    wS = [("dve", vinvl[t]), ("lxn%d" % tp, vxnl2[t]),
                          ("sy%d" % tp, prev_fin_dma[tp] if t < 2 else vfdma[t - 2])]
                    vS_[t] = pC.add("vector", lambda E, S, t=t, tp=tp: E.scalar_tensor_tensor(s_sb[:, tp, :], acc[:, t, :], invl[:, t:t + 1], xn_ld[:, tp, :], ALU.mult, ALU.add),
                                    waits=wS, inc="dve")
                    wY = [vG.get(t - 1, prev_psVC_free)]
                    if t == 0:
                        wY.append(("dve", vUP[K - 1]))
                        if s == 0:
                            wY += [("ldb", vbr), ("dve", vones)]

                    def fnY(E, S, t=t, sp=sp):
                        last = None
                        for h in range(2):
                            for d in range(8):
                                E.matmul(psVC[:, h * 512:(h + 1) * 512], xnT_own[:, sp, d, t * 128:(t + 1) * 128],
                                         M_sb[:, d, h * 512:(h + 1) * 512], start=(d == 0), stop=False)
                            last = E.matmul(psVC[:, h * 512:(h + 1) * 512], ones1f[0:1, :], biasrow[0:1, h * 512:(h + 1) * 512],
                                            start=False, stop=True)
                        return last
                    vY[t] = pC.add("tensor", fnY, waits=[w for w in wY if w], inc="pe")
                    vG[t] = ("act", pC.add("scalar", lambda E, S, tp=tp: E.activation(gsil[:, tp, :], psVC[:], AF.Silu),
                                           waits=[("pe", vY[t]), ("dve", vMU.get(t - 2, prev_pool_bypar[tp]))], inc="act"))
                    vMU[t] = pC.add("vector", lambda E, S, tp=tp: E.tensor_mul(yb_sb[:, tp, :], s_sb[:, tp, :], gsil[:, tp, :]),
                                    waits=[vG[t], ("dve", vS_[t])], inc="dve")
                    vOUT[t] = pC.add("vector", lambda E, S, tp=tp, t4=t4: E.tensor_add(s_sb[:, tp, :], yb_sb[:, tp, :], xstC[:, t4, :]),
                                     waits=[("dve", vMU[t]), ("lxf%d" % t4, vXF[t])], inc="dve")
                    vfdma[t] = pC.add("gpsimd", lambda E, S, s=s, t=t, tp=tp: E.dma_start(out=yown_t[:, s, t, :], in_=s_sb[:, tp, :]),
                                      waits=[("dve", vOUT[t])], inc="sy%d" % tp, by=16)
                    if t + 2 < 8:
                        vxnl2[t + 2] = pC.add("scalar", lambda E, S, s=s, t=t: E.dma_start(out=xn_ld[:, t % 2, :], in_=xn2scr_t[:, s, t + 2, :]),
                                              waits=[("dve", vS_[t])], inc="lxn%d" % (t % 2), by=16)
                    if t + 4 < 8:
                        vXF[t + 4] = pC.add("scalar", lambda E, S, s=s, t=t: E.dma_start(out=xstC[:, t % 4, :], in_=xown_t[:, s, t + 4, :]),
                                            waits=[("dve", vOUT[t])], inc="lxf%d" % (t % 4), by=16)

                fin_pe_bypar[sp] = pC.val("pe")
                prev_psVC_free = vG[7]
                prev_fin_dma = {0: vfdma[6], 1: vfdma[7]}
                prev_out4 = {0: vOUT[4], 1: vOUT[5], 2: vOUT[6], 3: vOUT[7]}
                prev_Sfree_bypar = {0: vS_[6], 1: vS_[7]}
                prev_pool_bypar = {0: vMU[6], 1: vMU[7]}
                prev_ex_bybank = {0: vEX[K - 2], 1: vEX[K - 1]}
                prev_tr_bypar = {0: vTR[K - 2], 1: vTR[K - 1]}
                prev_evt_last = vEVT[K - 1]

            pC.add("gpsimd", lambda E, S: None, waits=[("sy0", pC.val("sy0")), ("sy1", pC.val("sy1"))])
            dispatch(pC)

    return nc


def _host_consts():
    k = np.arange(D)
    ang = 2.0 * np.pi * (np.outer(k, k) % D).astype(np.float64) / D
    bf = ml_dtypes.bfloat16
    consts = {
        "Fr_h": np.cos(ang).astype(bf),
        "Fi_h": (-np.sin(ang)).astype(bf),
        "Gr_h": (np.cos(ang) / D).astype(bf),
        "Gin_h": (-np.sin(ang) / D).astype(bf),
        "ident_h": np.eye(128, dtype=bf),
    }
    tri = np.where(np.arange(128)[None, :] <= np.arange(128)[:, None], 0.0, NEG_BIG)
    consts["trimask_h"] = tri.astype(bf)
    return consts


def _btable(lane, perm):
    bt = np.zeros((128, NITER), dtype=np.float32)
    for s in range(NSLOT):
        c = chunk_of(lane, s)
        for j in range(LTRIP[s]):
            if perm[j] >= c:
                bt[:, SLOT_OFF[s] + j] = -NEG_BIG
    return bt


def kernel(L0_attn_x, L0_attn_W, L0_ffn_w_freq_real, L0_ffn_w_freq_imag, L0_ffn_bias):
    x = np.ascontiguousarray(np.asarray(L0_attn_x, dtype=np.float32))
    W = np.asarray(L0_attn_W, dtype=np.float32)
    wfr = np.ascontiguousarray(np.asarray(L0_ffn_w_freq_real, dtype=np.float32))
    wfi = np.ascontiguousarray(np.asarray(L0_ffn_w_freq_imag, dtype=np.float32))
    bias = np.ascontiguousarray(np.asarray(L0_ffn_bias, dtype=np.float32))
    WTh = np.ascontiguousarray(W.T)

    consts = _host_consts()
    nc = build_nc()
    in_maps = []
    for core in range(8):
        b, lane = core // 4, core % 4
        xb = x[b]
        perm = perm_of(lane)
        xb_perm = np.ascontiguousarray(
            xb.reshape(16, 1024, D)[perm].reshape(16 * 1024, D))
        xown = np.stack([xb[chunk_of(lane, s) * 1024:(chunk_of(lane, s) + 1) * 1024] for s in range(NSLOT)])
        m = {
            "x_all": xb_perm,
            "x_own": np.ascontiguousarray(xown),
            "WTh": WTh,
            "wfr": wfr,
            "wfi": wfi,
            "bias_h": bias,
            "btab_h": _btable(lane, perm),
        }
        m.update(consts)
        in_maps.append(m)

    res = run_bass_kernel_spmd(nc, in_maps, core_ids=list(range(8)))
    out = np.empty((2, 16384, D), dtype=np.float32)
    for core in range(8):
        b, lane = core // 4, core % 4
        y = res.results[core]["y_own"]
        for s in range(NSLOT):
            c = chunk_of(lane, s)
            out[b, c * 1024:(c + 1) * 1024] = y[s]
    return out


# version 13
# speedup vs baseline: 1.0109x; 1.0109x over previous
import numpy as np
import ml_dtypes
import contextlib
import concourse.bass as bass
import concourse.mybir as mybir
from concourse.bass_utils import run_bass_kernel_spmd

F32 = mybir.dt.float32
BF16 = mybir.dt.bfloat16
AF = mybir.ActivationFunctionType
ALU = mybir.AluOpType
AX = mybir.AxisListType

D = 1024
NCH = 16
NSLOT = 4
LTRIP = (15, 11, 7, 3)
POS_OWN = (15, 11, 7, 3)
SLOT_OFF = (0, 15, 26, 33)
NITER = 36
NEG_BIG = -30000.0
EPS = 1e-5
NBLK = NCH
PORDER = [15, 11, 7, 3] + [p for p in range(16) if p not in (15, 11, 7, 3)]
POS2SLOT = {POS_OWN[s]: s for s in range(NSLOT)}

ENGS = ("sync", "gpsimd", "tensor", "vector", "scalar")
SEMS = ("lda", "ldf", "lds", "ldb", "ldw", "ldg", "ldg2", "stq", "stq2",
        "ldh0", "ldh1", "sst0", "sst1", "sxn2", "lck0", "lck1", "lck1g",
        "lck1s", "lcx0", "lcx1", "lxn0", "lxn1", "lxf0", "lxf1", "lxf2",
        "lxf3", "sy0", "sy1", "pe", "act", "dve")


def chunk_of(lane, s):
    return 4 * (3 - s) + ((lane + s) % 4)


def perm_of(lane):
    own = {POS_OWN[s]: chunk_of(lane, s) for s in range(NSLOT)}
    rest = sorted(set(range(16)) - set(own.values()))
    perm = []
    ri = 0
    for p in range(16):
        if p in own:
            perm.append(own[p])
        else:
            perm.append(rest[ri])
            ri += 1
    for s in range(NSLOT):
        c = chunk_of(lane, s)
        assert set(range(c)) <= set(perm[:LTRIP[s]]), (lane, s, perm)
    return perm


class Plan:
    def __init__(self, cnt):
        self.ops = {e: [] for e in ENGS}
        self.c = cnt

    def val(self, s):
        return self.c.get(s, 0)

    def add(self, eng, fn, waits=(), inc=None, by=1):
        v = None
        if inc is not None:
            v = self.c.get(inc, 0) + by
            self.c[inc] = v
        waits = tuple((s, w) for (s, w) in waits if w and w > 0)

        def run(E, S, fn=fn, waits=waits, inc=inc, by=by):
            for s, w in waits:
                E.wait_ge(S[s], w)
            ins = fn(E, S)
            if inc is not None:
                assert ins is not None
                ins.then_inc(S[inc], by)
        self.ops[eng].append(run)
        return v


def build_nc():
    nc = bass.Bass("TRN2", target_bir_lowering=False, detect_race_conditions=False)

    x_all = nc.dram_tensor("x_all", [NCH * 1024, D], F32, kind="ExternalInput").ap()
    x_own = nc.dram_tensor("x_own", [NSLOT, 1024, D], F32, kind="ExternalInput").ap()
    WTh = nc.dram_tensor("WTh", [D, 2 * D], F32, kind="ExternalInput").ap()
    wfr = nc.dram_tensor("wfr", [D, D], F32, kind="ExternalInput").ap()
    wfi = nc.dram_tensor("wfi", [D, D], F32, kind="ExternalInput").ap()
    bias_h = nc.dram_tensor("bias_h", [D], F32, kind="ExternalInput").ap()
    Fr_h = nc.dram_tensor("Fr_h", [D, D], BF16, kind="ExternalInput").ap()
    Fi_h = nc.dram_tensor("Fi_h", [D, D], BF16, kind="ExternalInput").ap()
    Gr_h = nc.dram_tensor("Gr_h", [D, D], BF16, kind="ExternalInput").ap()
    Gin_h = nc.dram_tensor("Gin_h", [D, D], BF16, kind="ExternalInput").ap()
    ident_h = nc.dram_tensor("ident_h", [128, 128], BF16, kind="ExternalInput").ap()
    trimask_h = nc.dram_tensor("trimask_h", [128, 128], BF16, kind="ExternalInput").ap()
    btab_h = nc.dram_tensor("btab_h", [128, NITER], F32, kind="ExternalInput").ap()
    y_own = nc.dram_tensor("y_own", [NSLOT, 1024, D], F32, kind="ExternalOutput").ap()

    KT_scr = nc.dram_tensor("KT_scr", [D, NBLK * 1024], BF16, kind="Internal").ap()
    V_scr = nc.dram_tensor("V_scr", [NBLK * 1024, D], BF16, kind="Internal").ap()
    xnT_scr = nc.dram_tensor("xnT_scr", [NSLOT, 8, 128, 1024], BF16, kind="Internal").ap()
    xn2_scr = nc.dram_tensor("xn2_scr", [NSLOT, 8, 128, D], BF16, kind="Internal").ap()

    xall_t = x_all.rearrange("(n p) d -> p n d", p=128)
    xown_t = x_own.rearrange("s (n p) d -> p s n d", p=128)
    ktscr_t = KT_scr.rearrange("(a p) t -> p a t", p=128)
    vscr_t = V_scr.rearrange("(n p) d -> p n d", p=128)
    xnscr_t = xnT_scr.rearrange("s c p t -> p s c t")
    xn2scr_t = xn2_scr.rearrange("s q p d -> p s q d")
    yown_t = y_own.rearrange("s (n p) d -> p s n d", p=128)

    cnt = {}
    with contextlib.ExitStack() as st:
        S = {}
        for name in SEMS:
            S[name] = st.enter_context(nc.semaphore(name))

        def dispatch(plan):
            with nc.Block() as blk:
                for e in ENGS:
                    ops = plan.ops[e]
                    if not ops:
                        continue

                    def body(E, ops=ops):
                        for f in ops:
                            f(E, S)
                    getattr(blk, e)(body)

        M_sb = st.enter_context(nc.sbuf_tensor("M_sb", [128, 8, D], BF16))
        bias_bc = st.enter_context(nc.sbuf_tensor("bias_bc", [128, D], F32))
        ident = st.enter_context(nc.sbuf_tensor("ident", [128, 128], BF16))
        trimask = st.enter_context(nc.sbuf_tensor("trimask", [128, 128], BF16))
        btab = st.enter_context(nc.sbuf_tensor("btab", [128, NITER], F32))
        cstk = st.enter_context(nc.sbuf_tensor("cstk", [128, 4], F32))

        with contextlib.ExitStack() as stAB:
            WT_sb = stAB.enter_context(nc.sbuf_tensor("WT_sb", [128, 8, 2 * D], BF16))
            xstB = stAB.enter_context(nc.sbuf_tensor("xstB", [128, 2, 4, D], F32))
            xn_bf = stAB.enter_context(nc.sbuf_tensor("xn_bf", [128, 8, D], BF16))
            bns = stAB.enter_context(nc.sbuf_tensor("bns", [128, 8, 12], F32))
            mv2 = stAB.enter_context(nc.sbuf_tensor("mv2", [128, 8, 2], F32))
            sv = stAB.enter_context(nc.sbuf_tensor("sv", [128, 8], F32))
            r8 = stAB.enter_context(nc.sbuf_tensor("r8", [128, 8], F32))

            v_xn = {}
            v_evxnT = {}
            v_store = {}
            v_xn2st = {}
            vevK = {}
            vevV = {}
            v_half = {}
            v_trB = {}

            def emit_ln_stats(plan, idx):
                vst2 = {}
                for q in range(8):
                    hid = 2 * idx + q // 4
                    hp = hid % 2
                    xsrc = xstB[:, hp, q % 4, :]
                    w = [("ldh%d" % hp, v_half[hid])] if q % 4 == 0 else []
                    plan.add("vector", lambda E, S, xsrc=xsrc, q=q: E.bn_stats(bns[:, q, 0:6], xsrc[:, 0:512]), waits=w)
                    vst2[q] = plan.add("vector", lambda E, S, xsrc=xsrc, q=q: E.bn_stats(bns[:, q, 6:12], xsrc[:, 512:1024]), inc="dve")
                va = None
                for q in range(8):
                    va = plan.add("vector", lambda E, S, q=q: E.bn_aggr(mv2[:, q, :], bns[:, q, :]),
                                  waits=[("dve", vst2[q])], inc="dve")
                return va

            def emit_ln_fin(plan, va):
                vsq = plan.add("scalar", lambda E, S: E.activation(sv[:, 0:8], mv2[:, :, 1:2].rearrange("p q o -> p (q o)"),
                                                                  AF.Sqrt, bias=cstk[:, 0:1]),
                               waits=[("dve", va)], inc="act")
                vrc = plan.add("vector", lambda E, S: E.reciprocal(r8[:, 0:8], sv[:, 0:8]), waits=[("act", vsq)], inc="dve")
                return vrc

            def emit_ln_ts(plan, idx, vrc):
                for q in range(8):
                    ti = idx * 8 + q
                    hid = 2 * idx + q // 4
                    hp = hid % 2
                    xsrc = xstB[:, hp, q % 4, :]
                    w = [("dve", vrc), ("pe", v_trB.get(ti - 8, 0)), ("sxn2", v_xn2st.get(idx - 1, 0))]
                    v_xn[ti] = plan.add("vector", lambda E, S, xsrc=xsrc, q=q: E.tensor_scalar(
                        xn_bf[:, q, :], xsrc, mv2[:, q, 0:1], r8[:, q:q + 1], ALU.subtract, ALU.mult),
                        waits=w, inc="dve")

            pA = Plan(cnt)
            with contextlib.ExitStack() as stA:
                ArT = stA.enter_context(nc.sbuf_tensor("ArT", [128, 8, D], BF16))
                AiT = stA.enter_context(nc.sbuf_tensor("AiT", [128, 8, D], BF16))
                psA0 = stA.enter_context(nc.psum_tensor("psA0", [128, D], F32))
                psA1 = stA.enter_context(nc.psum_tensor("psA1", [128, D], F32))
                psA = (psA0, psA1)

                pos0 = PORDER[0]
                for h in range(2):
                    src_ = xall_t[:, pos0 * 8 + h * 4:pos0 * 8 + h * 4 + 4, :]
                    v_half[h] = pA.add("scalar", lambda E, S, src_=src_, h=h: E.dma_start(out=xstB[:, h, :, :], in_=src_),
                                       inc="ldh%d" % h, by=16)

                pA.add("vector", lambda E, S: E.memset(cstk[:, 0:1], EPS))
                pA.add("vector", lambda E, S: E.memset(cstk[:, 1:2], -0.5))
                pA.add("vector", lambda E, S: E.memset(cstk[:, 2:3], -1.0), inc="dve")
                va0 = emit_ln_stats(pA, 0)

                with contextlib.ExitStack() as stA1:
                    Fr_sb = stA1.enter_context(nc.sbuf_tensor("Fr_sb", [128, 8, D], BF16))
                    Fi_sb = stA1.enter_context(nc.sbuf_tensor("Fi_sb", [128, 8, D], BF16))
                    wr_bf = stA1.enter_context(nc.sbuf_tensor("wr_bf", [128, 8, D], BF16))
                    wi_bf = stA1.enter_context(nc.sbuf_tensor("wi_bf", [128, 8, D], BF16))

                    vFr = pA.add("sync", lambda E, S: E.dma_start(out=Fr_sb[:], in_=Fr_h.rearrange("(c p) t -> p c t", p=128)), inc="ldf", by=16)
                    pA.add("sync", lambda E, S: E.dma_start(out=ident[:], in_=ident_h), inc="lda", by=16)
                    pA.add("sync", lambda E, S: E.dma_start(out=trimask[:], in_=trimask_h), inc="lda", by=16)
                    vlda = pA.add("sync", lambda E, S: E.dma_start(out=btab[:], in_=btab_h), inc="lda", by=16)
                    vFi = pA.add("scalar", lambda E, S: E.dma_start(out=Fi_sb[:], in_=Fi_h.rearrange("(c p) t -> p c t", p=128)), inc="lds", by=16)
                    vwi = pA.add("gpsimd", lambda E, S: E.dma_start(out=wi_bf[:], in_=wfi.rearrange("(c p) o -> p c o", p=128)), inc="stq", by=16)
                    vwr = pA.add("gpsimd", lambda E, S: E.dma_start(out=wr_bf[:], in_=wfr.rearrange("(c p) o -> p c o", p=128)), inc="stq2", by=16)
                    vWT = pA.add("gpsimd", lambda E, S: E.dma_start(out=WT_sb[:], in_=WTh.rearrange("(c p) f -> p c f", p=128)),
                                 inc="ldw", by=16)

                    evA = []
                    vneg = None
                    vmmlast = None
                    for grp in range(2):
                        if grp == 0:
                            dstT, l0, l1 = AiT, wi_bf, wr_bf
                            w_a = [("stq", vwi), ("ldf", vFr)]
                            w_b = [("stq2", vwr), ("lds", vFi)]
                        else:
                            dstT, l0, l1 = ArT, wr_bf, wi_bf
                            w_a = [("stq2", vwr), ("ldf", vFr)]
                            w_b = [("dve", vneg), ("lds", vFi)]
                        for ot in range(8):
                            idx = grp * 8 + ot
                            ps = psA[idx % 2]
                            wa = list(w_a)
                            if idx >= 2:
                                wa.append(("act", evA[idx - 2]))

                            def mm_p1(E, S, ps=ps, l0=l0, ot=ot):
                                last = None
                                for h in range(2):
                                    for d in range(8):
                                        last = E.matmul(
                                            ps[:, h * 512:(h + 1) * 512],
                                            l0[:, d, ot * 128:(ot + 1) * 128],
                                            Fr_sb[:, d, h * 512:(h + 1) * 512],
                                            start=(d == 0), stop=False)
                                return last
                            pA.add("tensor", mm_p1, waits=wa, inc="pe")

                            def mm_p2(E, S, ps=ps, l1=l1, ot=ot):
                                last = None
                                for h in range(2):
                                    for d in range(8):
                                        last = E.matmul(
                                            ps[:, h * 512:(h + 1) * 512],
                                            l1[:, d, ot * 128:(ot + 1) * 128],
                                            Fi_sb[:, d, h * 512:(h + 1) * 512],
                                            start=False, stop=(d == 7))
                                return last
                            vmmlast = pA.add("tensor", mm_p2, waits=list(w_b), inc="pe")
                            evA.append(pA.add("scalar", lambda E, S, ps=ps, dstT=dstT, ot=ot: E.copy(dstT[:, ot, :], ps[:]),
                                              waits=[("pe", vmmlast)], inc="act"))
                        if grp == 0:
                            vneg = pA.add("vector", lambda E, S: E.tensor_scalar_mul(wi_bf[:], wi_bf[:], -1.0),
                                          waits=[("pe", vmmlast)], inc="dve")
                            vrc0 = emit_ln_fin(pA, va0)
                    vA_ev_done = evA[-1]
                    emit_ln_ts(pA, 0, vrc0)

                with contextlib.ExitStack() as stA2:
                    Gr_sb = stA2.enter_context(nc.sbuf_tensor("Gr_sb", [128, 8, D], BF16))
                    Gin_sb = stA2.enter_context(nc.sbuf_tensor("Gin_sb", [128, 8, D], BF16))
                    vGr = pA.add("sync", lambda E, S: E.dma_start(out=Gr_sb[:], in_=Gr_h.rearrange("(c p) t -> p c t", p=128)),
                                 waits=[("pe", vmmlast)], inc="ldg", by=16)
                    vGin = pA.add("sync", lambda E, S: E.dma_start(out=Gin_sb[:], in_=Gin_h.rearrange("(c p) t -> p c t", p=128)),
                                  inc="ldg2", by=16)

                    evM = []
                    for tt in range(8):
                        ps = psA[tt % 2]
                        wa = [("ldg", vGr), ("act", vA_ev_done)]
                        if tt >= 2:
                            wa.append(("act", evM[tt - 2]))

                        def mm_a(E, S, ps=ps, tt=tt):
                            last = None
                            for h in range(2):
                                for k in range(8):
                                    last = E.matmul(ps[:, h * 512:(h + 1) * 512],
                                                    ArT[:, k, tt * 128:(tt + 1) * 128],
                                                    Gr_sb[:, k, h * 512:(h + 1) * 512],
                                                    start=(k == 0), stop=False)
                            return last
                        pA.add("tensor", mm_a, waits=wa, inc="pe")

                        def mm_b(E, S, ps=ps, tt=tt):
                            last = None
                            for h in range(2):
                                for k in range(8):
                                    last = E.matmul(ps[:, h * 512:(h + 1) * 512],
                                                    AiT[:, k, tt * 128:(tt + 1) * 128],
                                                    Gin_sb[:, k, h * 512:(h + 1) * 512],
                                                    start=False, stop=(k == 7))
                            return last
                        vmm = pA.add("tensor", mm_b, waits=[("ldg2", vGin)], inc="pe")
                        evM.append(pA.add("scalar", lambda E, S, ps=ps, tt=tt: E.copy(M_sb[:, tt, :], ps[:]),
                                          waits=[("pe", vmm)], inc="act"))
                vA_act = pA.val("act")
                vA_pe = pA.val("pe")
                vA_dve = pA.val("dve")
                dispatch(pA)

            pB = Plan(cnt)
            barrier = (("act", vA_act), ("pe", vA_pe), ("dve", vA_dve))
            with contextlib.ExitStack() as stB:
                with contextlib.ExitStack() as stB0:
                    biasrow = stB0.enter_context(nc.sbuf_tensor("biasrow", [1, D], F32))
                    ones1f = stB0.enter_context(nc.sbuf_tensor("ones1f", [1, 128], F32))
                    psB0 = stB0.enter_context(nc.psum_tensor("psB0", [128, D], F32))

                    vbr = pB.add("sync", lambda E, S: E.dma_start(out=biasrow[:], in_=bias_h[None, :]), waits=barrier, inc="ldb", by=16)
                    vones = pB.add("vector", lambda E, S: E.memset(ones1f[:], 1.0), waits=barrier, inc="dve")

                    def mmbias(E, S):
                        E.matmul(psB0[:, 0:512], ones1f[0:1, :], biasrow[0:1, 0:512], start=True, stop=True)
                        return E.matmul(psB0[:, 512:1024], ones1f[0:1, :], biasrow[0:1, 512:1024], start=True, stop=True)
                    vbb = pB.add("tensor", mmbias, waits=barrier + (("ldb", vbr), ("dve", vones)), inc="pe")
                    pB.add("vector", lambda E, S: E.tensor_copy(bias_bc[:], psB0[:]), waits=[("pe", vbb)], inc="dve")
                    vB0_dve = pB.val("dve")
                    vB0_pe = pB.val("pe")

                xnT_B = stB.enter_context(nc.sbuf_tensor("xnT_B", [128, 2, 8, D], BF16))
                KT_blk = stB.enter_context(nc.sbuf_tensor("KT_blk", [128, 2, 8, D], BF16, side="right"))
                V_blk = stB.enter_context(nc.sbuf_tensor("V_blk", [128, 2, 8, D], BF16, side="right"))
                psK0 = stB.enter_context(nc.psum_tensor("psK0", [128, D], F32))
                psK1 = stB.enter_context(nc.psum_tensor("psK1", [128, D], F32))
                psV0 = stB.enter_context(nc.psum_tensor("psV0", [128, D], F32))
                psV1 = stB.enter_context(nc.psum_tensor("psV1", [128, D], F32))
                psK = (psK0, psK1)
                psV = (psV0, psV1)

                def emit_halves(idx):
                    pos = PORDER[idx]
                    for h in range(2):
                        hid = 2 * idx + h
                        hp = hid % 2
                        src_ = xall_t[:, pos * 8 + h * 4:pos * 8 + h * 4 + 4, :]
                        w = [("dve", v_xn.get((hid - 2) * 4 + 3, 0))]
                        v_half[hid] = pB.add("sync", lambda E, S, src_=src_, hp=hp: E.dma_start(out=xstB[:, hp, :, :], in_=src_),
                                             waits=w, inc="ldh%d" % hp, by=16)

                def emit_xn2(idx):
                    s_ = POS2SLOT[PORDER[idx]]
                    v_xn2st[idx] = pB.add("gpsimd", lambda E, S, s_=s_: E.dma_start(out=xn2scr_t[:, s_, :, :], in_=xn_bf[:]),
                                          waits=[("dve", v_xn[idx * 8 + 7])], inc="sxn2", by=16)

                def emit_tr(idx):
                    jp = idx % 2
                    regs = (psK[0], psK[1], psV[0], psV[1])
                    for q in range(8):
                        ti = idx * 8 + q
                        r4 = q % 4
                        wtr = [("dve", v_xn[ti]), ("lda", vlda)]
                        if r4 < 2:
                            wtr.append(("act", vevK.get((idx - 1, 6 + r4), 0)))
                        else:
                            wtr.append(("act", vevV.get((idx - 1, 4 + r4), 0)))
                        pv = v_evxnT.get(ti - 4)
                        if pv is not None:
                            wtr.append((pv[0], pv[1]))
                        if ti == 0:
                            wtr += [("pe", vB0_pe), ("dve", vB0_dve)]

                        def trB(E, S, q=q, r4=r4, regs=regs):
                            last = None
                            for d in range(8):
                                last = E.matmul(regs[r4][:, d * 128:(d + 1) * 128], xn_bf[:, q, d * 128:(d + 1) * 128],
                                                ident[:], start=True, stop=True)
                            return last
                        vtr = pB.add("tensor", trB, waits=wtr, inc="pe")
                        v_trB[ti] = vtr
                        wev = [("pe", vtr)]
                        if q == 0:
                            wev.append(("sst%d" % jp, v_store.get(idx - 2, 0)))

                        def evxnT_v(E, S, r4=r4, jp=jp, q=q, regs=regs):
                            return E.tensor_copy(xnT_B[:, jp, :, q * 128:(q + 1) * 128], regs[r4].rearrange("p (c t) -> p c t", c=8))

                        def evxnT_s(E, S, r4=r4, jp=jp, q=q, regs=regs):
                            return E.copy(xnT_B[:, jp, :, q * 128:(q + 1) * 128], regs[r4].rearrange("p (c t) -> p c t", c=8))
                        if q % 2 == 0:
                            v_evxnT[ti] = ("dve", pB.add("vector", evxnT_v, waits=wev, inc="dve"))
                        else:
                            v_evxnT[ti] = ("act", pB.add("scalar", evxnT_s, waits=wev, inc="act"))

                def _evw(idx, qmax):
                    wd = wa = 0
                    for q in range(qmax + 1):
                        e, v = v_evxnT[idx * 8 + q]
                        if e == "dve":
                            wd = max(wd, v)
                        else:
                            wa = max(wa, v)
                    return [("dve", wd), ("act", wa)]

                def emit_kv(idx, f0, f1):
                    jp = idx % 2
                    for ftile in range(f0, f1):
                        for kind in range(2):
                            kp = ftile % 2
                            for h in range(2):
                                wmm = _evw(idx, 3 if h == 0 else 7)
                                if idx == 0 and ftile == 0 and h == 0:
                                    wmm.append(("ldw", vWT))
                                if h == 0:
                                    if kind == 0:
                                        if ftile >= 2:
                                            wmm.append(("act", vevK.get((idx, ftile - 2), 0)))
                                        else:
                                            e, v = v_evxnT[idx * 8 + 4 + kp]
                                            wmm.append((e, v))
                                    else:
                                        if ftile >= 2:
                                            wmm.append(("act", vevV.get((idx, ftile - 2), 0)))
                                        else:
                                            e, v = v_evxnT[idx * 8 + 6 + kp]
                                            wmm.append((e, v))
                                if kind == 0:
                                    def mmK(E, S, jp=jp, ft=ftile, kp=kp, h=h):
                                        last = None
                                        for d in range(8):
                                            last = E.matmul(psK[kp][:, h * 512:(h + 1) * 512],
                                                            WT_sb[:, d, ft * 128:(ft + 1) * 128],
                                                            xnT_B[:, jp, d, h * 512:(h + 1) * 512],
                                                            start=(d == 0), stop=(d == 7))
                                        return last
                                    vmm = pB.add("tensor", mmK, waits=wmm, inc="pe")
                                else:
                                    def mmV(E, S, jp=jp, tt=ftile, kp=kp, h=h):
                                        last = None
                                        for d in range(8):
                                            last = E.matmul(psV[kp][:, h * 512:(h + 1) * 512],
                                                            xnT_B[:, jp, d, tt * 128:(tt + 1) * 128],
                                                            WT_sb[:, d, D + h * 512:D + (h + 1) * 512],
                                                            start=(d == 0), stop=(d == 7))
                                        return last
                                    vmm = pB.add("tensor", mmV, waits=wmm, inc="pe")
                            wev = [("pe", vmm)]
                            if ftile < 2:
                                wev.append(("sst%d" % jp, v_store.get(idx - 2, 0)))
                            if kind == 0:
                                vevK[(idx, ftile)] = pB.add("scalar", lambda E, S, jp=jp, ft=ftile, kp=kp: E.copy(KT_blk[:, jp, ft, :], psK[kp][:]),
                                                            waits=wev, inc="act")
                            else:
                                vevV[(idx, ftile)] = pB.add("scalar", lambda E, S, jp=jp, tt=ftile, kp=kp: E.copy(V_blk[:, jp, tt, :], psV[kp][:]),
                                                            waits=wev, inc="act")

                def emit_stores(idx):
                    jp = idx % 2
                    pos = PORDER[idx]
                    sem = "sst%d" % jp
                    pB.add("gpsimd", lambda E, S, jp=jp, pos=pos: E.dma_start(out=ktscr_t[:, :, pos * 1024:(pos + 1) * 1024], in_=KT_blk[:, jp, :, :]),
                           waits=[("act", vevK[(idx, 7)])], inc=sem, by=16)
                    vst = pB.add("gpsimd", lambda E, S, jp=jp, pos=pos: E.dma_start(out=vscr_t[:, pos * 8:(pos + 1) * 8, :], in_=V_blk[:, jp, :, :]),
                                 waits=[("act", vevV[(idx, 7)])], inc=sem, by=16)
                    if pos in POS2SLOT:
                        s_ = POS2SLOT[pos]
                        e, v = v_evxnT[idx * 8 + 7]
                        vst = pB.add("gpsimd", lambda E, S, jp=jp, s_=s_: E.dma_start(out=xnscr_t[:, s_, :, :], in_=xnT_B[:, jp, :, :]),
                                     waits=[(e, v), ("dve", v_evxnT[idx * 8 + 6][1])], inc=sem, by=16)
                    v_store[idx] = vst

                for idx in range(NBLK):
                    if PORDER[idx] in POS2SLOT:
                        emit_xn2(idx)
                    emit_tr(idx)
                    emit_kv(idx, 0, 4)
                    if idx + 1 < NBLK:
                        emit_halves(idx + 1)
                        va = emit_ln_stats(pB, idx + 1)
                        vrc = emit_ln_fin(pB, va)
                    emit_kv(idx, 4, 8)
                    if idx + 1 < NBLK:
                        emit_ln_ts(pB, idx + 1, vrc)
                    emit_stores(idx)

                vB_act = pB.val("act")
                vB_pe = pB.val("pe")
                vB_dve = pB.val("dve")
                vB_sst0 = pB.val("sst0")
                vB_sst1 = pB.val("sst1")
                vB_sxn2 = pB.val("sxn2")
                vB_own = (("sst0", v_store[4]), ("sst1", v_store[3]), ("sxn2", vB_sxn2))
                dispatch(pB)

        pC = Plan(cnt)
        with contextlib.ExitStack() as stC:
            KTs = stC.enter_context(nc.sbuf_tensor("KTs", [128, 2, 8, D], BF16))
            Vs = stC.enter_context(nc.sbuf_tensor("Vs", [128, 2, 8, D], BF16))
            acc = stC.enter_context(nc.sbuf_tensor("acc", [128, 8, D], F32))
            P_sb = stC.enter_context(nc.sbuf_tensor("P_sb", [128, 2, D], BF16))
            PT_sb = stC.enter_context(nc.sbuf_tensor("PT_sb", [128, D], BF16))
            xnT_own = stC.enter_context(nc.sbuf_tensor("xnT_own", [128, 2, 8, D], BF16))
            xn_ld = stC.enter_context(nc.sbuf_tensor("xn_ld", [128, 2, D], BF16))
            xstC = stC.enter_context(nc.sbuf_tensor("xstC", [128, 4, D], F32))
            s_sb = stC.enter_context(nc.sbuf_tensor("s_sb", [128, 2, D], F32))
            yb_sb = stC.enter_context(nc.sbuf_tensor("yb_sb", [128, 2, D], F32))
            gsil = stC.enter_context(nc.sbuf_tensor("gsil", [128, 2, D], F32))
            negm = stC.enter_context(nc.sbuf_tensor("negm", [128, 2, 8], F32))
            lsm = stC.enter_context(nc.sbuf_tensor("lsm", [128, 2, 8], F32))
            bP = stC.enter_context(nc.sbuf_tensor("bP", [128, 2, 8], F32))
            rr = stC.enter_context(nc.sbuf_tensor("rr", [128, 2, 8], F32))
            tmpm = stC.enter_context(nc.sbuf_tensor("tmpm", [128, 8], F32))
            lpart = stC.enter_context(nc.sbuf_tensor("lpart", [128, 8], F32))
            invl = stC.enter_context(nc.sbuf_tensor("invl", [128, 8], F32))
            psS0 = stC.enter_context(nc.psum_tensor("psS0", [128, D], F32))
            psS1 = stC.enter_context(nc.psum_tensor("psS1", [128, D], F32))
            psT = stC.enter_context(nc.psum_tensor("psT", [128, D], F32))
            psVC = stC.enter_context(nc.psum_tensor("psVC", [128, D], F32))
            psS = (psS0, psS1)

            barrierB = (("act", vB_act), ("pe", vB_pe), ("dve", vB_dve))
            stqB = (("sst0", vB_sst0), ("sst1", vB_sst1), ("sxn2", vB_sxn2))
            last_rd_buf = {0: 0, 1: 0}
            fin_pe_bypar = {0: 0, 1: 0}
            prev_ex_bybank = {0: 0, 1: 0}
            prev_tr_bypar = {0: 0, 1: 0}
            prev_evt_last = 0
            prev_psVC_free = 0
            prev_fin_dma = {0: 0, 1: 0}
            prev_out4 = {0: 0, 1: 0, 2: 0, 3: 0}
            prev_Sfree_bypar = {0: 0, 1: 0}

            def emit_prep(s):
                sp = s % 2
                dblk = POS_OWN[s]
                xq = "scalar" if s == 0 else "sync"
                kq, ksem = ("scalar", "lck1s") if s == 0 else ("gpsimd", "lck1g")
                wp = [("pe", fin_pe_bypar[sp])]
                if s == 0:
                    wp += list(barrierB) + list(vB_own)
                v_xnl = pC.add(xq, lambda E, S, s=s, sp=sp: E.dma_start(out=xnT_own[:, sp, :, :], in_=xnscr_t[:, s, :, :]),
                               waits=wp, inc="lcx%d" % sp, by=16)
                wk = [("pe", last_rd_buf[0])]
                if s == 0:
                    wk += list(barrierB) + list(vB_own)
                pC.add("sync", lambda E, S, dblk=dblk: E.dma_start(out=KTs[:, 0, :, :], in_=ktscr_t[:, :, dblk * 1024:(dblk + 1) * 1024]),
                       waits=wk, inc="lck0", by=16)
                v_vl = pC.add("sync", lambda E, S, dblk=dblk: E.dma_start(out=Vs[:, 0, :, :], in_=vscr_t[:, dblk * 8:(dblk + 1) * 8, :]),
                              inc="lck0", by=16)
                wk1 = [("pe", last_rd_buf[1])]
                if s == 0:
                    wk1 += list(barrierB) + list(vB_own)
                pC.add(kq, lambda E, S: E.dma_start(out=KTs[:, 1, :, :], in_=ktscr_t[:, :, 0:1024]),
                       waits=wk1, inc=ksem, by=16)
                v_v1 = pC.add(kq, lambda E, S: E.dma_start(out=Vs[:, 1, :, :], in_=vscr_t[:, 0:8, :]),
                              inc=ksem, by=16)
                return {"xnl": ("lcx%d" % sp, v_xnl), "pair": {0: ("lck0", v_vl), 1: (ksem, v_v1)}}

            pending = emit_prep(0)

            for s in range(NSLOT):
                L = LTRIP[s]
                sp = s % 2
                v_xnl = pending["xnl"]
                v_ldpair = dict(pending["pair"])

                def emit_load(j):
                    bp = (j + 1) % 2
                    sem = "lck%d" % bp
                    wl = [("pe", last_rd_buf[bp])]
                    if s == 0 and j == 1:
                        wl += list(stqB)
                    pC.add("sync", lambda E, S, bp=bp, j=j: E.dma_start(out=KTs[:, bp, :, :], in_=ktscr_t[:, :, j * 1024:(j + 1) * 1024]),
                           waits=wl, inc=sem, by=16)
                    vv = pC.add("sync", lambda E, S, bp=bp, j=j: E.dma_start(out=Vs[:, bp, :, :], in_=vscr_t[:, j * 8:(j + 1) * 8, :]),
                               inc=sem, by=16)
                    v_ldpair[j + 1] = (sem, vv)

                steps = [("diag", 0, t) for t in range(8)]
                for j in range(L):
                    steps += [("stream", j + 1, t) for t in range(8)]
                K = len(steps)

                vSC = {}
                vTR = {}
                vPV = {}
                vST = {}
                vUP = {}
                vEX = {}
                vR = {}
                vEVT = {}

                def bufp_of(i):
                    return 0 if i == 0 else i % 2

                def emit_SC(k):
                    kind, i, t = steps[k]
                    bp = bufp_of(i)
                    ps = psS[k % 2]
                    w = []
                    if k >= 2:
                        w.append(("act", vEX[k - 2]))
                    else:
                        w.append(("act", prev_ex_bybank[k % 2]))
                        w.append(v_xnl)
                        w.append(v_ldpair[0])
                        w.append(("lda", vlda))
                    if t == 0 and i > 0:
                        w.append(v_ldpair[i])
                    if kind == "diag":
                        ncols = (t + 1) * 128

                        def fn(E, S, t=t, bp=bp, ncols=ncols, sp=sp, ps=ps):
                            last = None
                            tri_lo = ncols - 128
                            for h in range(2):
                                lo = h * 512
                                hi = min(ncols, (h + 1) * 512)
                                if hi <= lo:
                                    break
                                has_tri = (tri_lo >= lo) and (ncols <= (h + 1) * 512)
                                for d in range(8):
                                    last = E.matmul(ps[:, lo:hi], xnT_own[:, sp, d, t * 128:(t + 1) * 128],
                                                    KTs[:, bp, d, lo:hi], start=(d == 0),
                                                    stop=(d == 7 and not has_tri), skip_group_check=True)
                                if has_tri:
                                    last = E.matmul(ps[:, tri_lo:ncols], ident[:], trimask[:],
                                                    start=False, stop=True, skip_group_check=True)
                            return last
                    else:
                        def fn(E, S, t=t, bp=bp, sp=sp, ps=ps):
                            last = None
                            for h in range(2):
                                for d in range(8):
                                    last = E.matmul(ps[:, h * 512:(h + 1) * 512], xnT_own[:, sp, d, t * 128:(t + 1) * 128],
                                                    KTs[:, bp, d, h * 512:(h + 1) * 512], start=(d == 0), stop=(d == 7))
                            return last
                    vSC[k] = pC.add("tensor", fn, waits=w, inc="pe")

                def emit_stats(k):
                    kind, i, t = steps[k]
                    p, q = i % 2, 1 - i % 2
                    ps = psS[k % 2]
                    w = [("pe", vSC[k])]
                    if kind == "diag":
                        ncols = (t + 1) * 128
                        vST[k] = pC.add("vector", lambda E, S, p=p, t=t, ncols=ncols, ps=ps: E.reduce_max(negm[:, p, t:t + 1], ps[:, 0:ncols], axis=AX.X, negate=True),
                                        waits=w, inc="dve")
                    else:
                        itcol = SLOT_OFF[s] + (i - 1)
                        v1 = pC.add("vector", lambda E, S, t=t, ps=ps: E.reduce_max(tmpm[:, t:t + 1], ps[:], axis=AX.X, negate=True),
                                    waits=w, inc="dve")
                        v2 = pC.add("vector", lambda E, S, p=p, q=q, t=t, itcol=itcol: E.tensor_scalar(negm[:, p, t:t + 1], tmpm[:, t:t + 1], btab[:, itcol:itcol + 1], negm[:, q, t:t + 1], ALU.add, ALU.min),
                                    waits=[("dve", v1)], inc="dve")
                        vST[k] = pC.add("vector", lambda E, S, p=p, t=t, itcol=itcol: E.tensor_scalar_sub(bP[:, p, t:t + 1], negm[:, p, t:t + 1], btab[:, itcol:itcol + 1]),
                                        waits=[("dve", v2)], inc="dve")

                def emit_R_EX(k):
                    kind, i, t = steps[k]
                    p, q = i % 2, 1 - i % 2
                    pp = k % 2
                    ps = psS[pp]
                    ncols = (t + 1) * 128 if kind == "diag" else 1024
                    if kind == "stream":
                        vR[k] = pC.add("scalar", lambda E, S, p=p, q=q, t=t: E.activation(rr[:, p, t:t + 1], negm[:, q, t:t + 1], AF.Exp, bias=negm[:, p, t:t + 1], scale=cstk[:, 2:3]),
                                       waits=[("dve", vST[k])], inc="act")
                        bt = bP
                    else:
                        vR[k] = None
                        bt = negm

                    def fn(E, S, bt=bt, p=p, t=t, pp=pp, ncols=ncols, ps=ps):
                        return E.activation(P_sb[:, pp, 0:ncols], ps[:, 0:ncols], AF.Exp,
                                            bias=bt[:, p, t:t + 1], accum_out=lpart[:, t:t + 1])
                    wex = [("dve", vST[k]), ("pe", vTR.get(k - 2, prev_tr_bypar[pp]))]
                    vEX[k] = pC.add("scalar", fn, waits=wex, inc="act")

                def emit_TR(k):
                    kind, i, t = steps[k]
                    nsub = (t + 1) if kind == "diag" else 8
                    pp = k % 2
                    w = [("act", vEX[k])]
                    if k == 0:
                        w.append(("act", prev_evt_last))

                    def fn(E, S, nsub=nsub, pp=pp):
                        last = None
                        for jj in range(nsub):
                            last = E.matmul(psT[:, jj * 128:(jj + 1) * 128], P_sb[:, pp, jj * 128:(jj + 1) * 128],
                                            ident[:], start=True, stop=True)
                        return last
                    vTR[k] = pC.add("tensor", fn, waits=w, inc="pe")

                def emit_EVT(k):
                    kind, i, t = steps[k]
                    ncols = ((t + 1) if kind == "diag" else 8) * 128
                    vEVT[k] = pC.add("scalar", lambda E, S, ncols=ncols: E.copy(PT_sb[:, 0:ncols], psT[:, 0:ncols]),
                                     waits=[("pe", vTR[k])], inc="act")

                def emit_PV(k):
                    kind, i, t = steps[k]
                    nsub = (t + 1) if kind == "diag" else 8
                    bp = bufp_of(i)
                    w = [("act", vEVT[k]), ("dve", vUP.get(k - 1, prev_psVC_free))]
                    if t == 0 and i > 0:
                        w.append(v_ldpair[i])

                    def fn(E, S, nsub=nsub, bp=bp):
                        last = None
                        for jj in range(nsub):
                            for h in range(2):
                                last = E.matmul(psVC[:, h * 512:(h + 1) * 512], PT_sb[:, jj * 128:(jj + 1) * 128],
                                                Vs[:, bp, jj, h * 512:(h + 1) * 512], start=(jj == 0), stop=(jj == nsub - 1))
                        return last
                    vPV[k] = pC.add("tensor", fn, waits=w, inc="pe")
                    last_rd_buf[bp] = vPV[k]

                def emit_UP(k):
                    kind, i, t = steps[k]
                    p, q = i % 2, 1 - i % 2
                    w = [("pe", vPV[k])]
                    if kind == "diag":
                        pC.add("vector", lambda E, S, p=p, t=t: E.tensor_copy(lsm[:, p, t:t + 1], lpart[:, t:t + 1]), waits=w)
                        vUP[k] = pC.add("vector", lambda E, S, t=t: E.tensor_copy(acc[:, t, :], psVC[:]), inc="dve")
                    else:
                        w.append(("act", vR[k]))
                        pC.add("vector", lambda E, S, p=p, q=q, t=t: E.scalar_tensor_tensor(lsm[:, p, t:t + 1], lsm[:, q, t:t + 1], rr[:, p, t:t + 1], lpart[:, t:t + 1], ALU.mult, ALU.add),
                               waits=w)
                        vUP[k] = pC.add("vector", lambda E, S, p=p, t=t: E.scalar_tensor_tensor(acc[:, t, :], acc[:, t, :], rr[:, p, t:t + 1], psVC[:], ALU.mult, ALU.add),
                                        inc="dve")

                emit_SC(0)
                emit_stats(0)
                emit_R_EX(0)
                emit_SC(1)
                emit_stats(1)
                emit_R_EX(1)
                for k in range(K):
                    kind, i, t = steps[k]
                    emit_TR(k)
                    emit_EVT(k)
                    if k + 2 < K:
                        emit_SC(k + 2)
                        emit_stats(k + 2)
                        emit_R_EX(k + 2)
                    emit_PV(k)
                    emit_UP(k)
                    if t == 7:
                        nx = i + 2
                        if 2 <= nx <= L:
                            emit_load(nx - 1)

                if s + 1 < NSLOT:
                    pending = emit_prep(s + 1)

                lp = L % 2
                vS_ = {}
                vY = {}
                vYB = {}
                vG = {}
                vOUT = {}
                vfdma = {}
                vinvl = {}
                vxnl2 = {}
                vXF = {}
                for t in range(2):
                    vxnl2[t] = pC.add("scalar", lambda E, S, s=s, t=t: E.dma_start(out=xn_ld[:, t % 2, :], in_=xn2scr_t[:, s, t, :]),
                                      waits=[("dve", prev_Sfree_bypar[t % 2]), ("sxn2", vB_sxn2)], inc="lxn%d" % (t % 2), by=16)
                for t in range(4):
                    vXF[t] = pC.add("scalar", lambda E, S, s=s, t=t: E.dma_start(out=xstC[:, t % 4, :], in_=xown_t[:, s, t, :]),
                                    waits=[("dve", prev_out4[t % 4])], inc="lxf%d" % (t % 4), by=16)
                for t in range(8):
                    tp = t % 2
                    t4 = t % 4
                    vinvl[t] = pC.add("vector", lambda E, S, t=t, lp=lp: E.reciprocal(invl[:, t:t + 1], lsm[:, lp, t:t + 1]),
                                      waits=[("dve", vUP[K - 8 + t])], inc="dve")
                    wS = [("dve", vinvl[t]), ("lxn%d" % tp, vxnl2[t]),
                          ("sy%d" % tp, prev_fin_dma[tp] if t < 2 else vfdma[t - 2])]
                    vS_[t] = pC.add("vector", lambda E, S, t=t, tp=tp: E.scalar_tensor_tensor(s_sb[:, tp, :], acc[:, t, :], invl[:, t:t + 1], xn_ld[:, tp, :], ALU.mult, ALU.add),
                                    waits=wS, inc="dve")
                    wY = [("dve", vYB.get(t - 1, 0))]
                    if t == 0:
                        wY.append(("dve", vUP[K - 1]))

                    def fnY(E, S, t=t, sp=sp):
                        last = None
                        for h in range(2):
                            for d in range(8):
                                last = E.matmul(psVC[:, h * 512:(h + 1) * 512], xnT_own[:, sp, d, t * 128:(t + 1) * 128],
                                                M_sb[:, d, h * 512:(h + 1) * 512], start=(d == 0), stop=(d == 7))
                        return last
                    vY[t] = pC.add("tensor", fnY, waits=wY, inc="pe")
                    vYB[t] = pC.add("vector", lambda E, S, tp=tp: E.tensor_add(yb_sb[:, tp, :], psVC[:], bias_bc[:]),
                                    waits=[("pe", vY[t])], inc="dve")
                    vG[t] = pC.add("scalar", lambda E, S, tp=tp: E.activation(gsil[:, tp, :], yb_sb[:, tp, :], AF.Silu),
                                   waits=[("dve", vYB[t])], inc="act")
                    vMU = pC.add("vector", lambda E, S, tp=tp: E.tensor_mul(yb_sb[:, tp, :], s_sb[:, tp, :], gsil[:, tp, :]),
                                 waits=[("act", vG[t])], inc="dve")
                    vOUT[t] = pC.add("vector", lambda E, S, tp=tp, t4=t4: E.tensor_add(s_sb[:, tp, :], yb_sb[:, tp, :], xstC[:, t4, :]),
                                     waits=[("dve", vMU), ("lxf%d" % t4, vXF[t])], inc="dve")
                    vfdma[t] = pC.add("gpsimd", lambda E, S, s=s, t=t, tp=tp: E.dma_start(out=yown_t[:, s, t, :], in_=s_sb[:, tp, :]),
                                      waits=[("dve", vOUT[t])], inc="sy%d" % tp, by=16)
                    if t + 2 < 8:
                        vxnl2[t + 2] = pC.add("scalar", lambda E, S, s=s, t=t: E.dma_start(out=xn_ld[:, t % 2, :], in_=xn2scr_t[:, s, t + 2, :]),
                                              waits=[("dve", vS_[t])], inc="lxn%d" % (t % 2), by=16)
                    if t + 4 < 8:
                        vXF[t + 4] = pC.add("scalar", lambda E, S, s=s, t=t: E.dma_start(out=xstC[:, t % 4, :], in_=xown_t[:, s, t + 4, :]),
                                            waits=[("dve", vOUT[t])], inc="lxf%d" % (t % 4), by=16)

                fin_pe_bypar[sp] = pC.val("pe")
                prev_psVC_free = vYB[7]
                prev_fin_dma = {0: vfdma[6], 1: vfdma[7]}
                prev_out4 = {0: vOUT[4], 1: vOUT[5], 2: vOUT[6], 3: vOUT[7]}
                prev_Sfree_bypar = {0: vS_[6], 1: vS_[7]}
                prev_ex_bybank = {0: vEX[K - 2], 1: vEX[K - 1]}
                prev_tr_bypar = {0: vTR[K - 2], 1: vTR[K - 1]}
                prev_evt_last = vEVT[K - 1]

            pC.add("gpsimd", lambda E, S: None, waits=[("sy0", pC.val("sy0")), ("sy1", pC.val("sy1"))])
            dispatch(pC)

    return nc


def _host_consts():
    k = np.arange(D)
    ang = 2.0 * np.pi * (np.outer(k, k) % D).astype(np.float64) / D
    bf = ml_dtypes.bfloat16
    consts = {
        "Fr_h": np.cos(ang).astype(bf),
        "Fi_h": (-np.sin(ang)).astype(bf),
        "Gr_h": (np.cos(ang) / D).astype(bf),
        "Gin_h": (-np.sin(ang) / D).astype(bf),
        "ident_h": np.eye(128, dtype=bf),
    }
    tri = np.where(np.arange(128)[None, :] <= np.arange(128)[:, None], 0.0, NEG_BIG)
    consts["trimask_h"] = tri.astype(bf)
    return consts


def _btable(lane, perm):
    bt = np.zeros((128, NITER), dtype=np.float32)
    for s in range(NSLOT):
        c = chunk_of(lane, s)
        for j in range(LTRIP[s]):
            if perm[j] >= c:
                bt[:, SLOT_OFF[s] + j] = -NEG_BIG
    return bt


def kernel(L0_attn_x, L0_attn_W, L0_ffn_w_freq_real, L0_ffn_w_freq_imag, L0_ffn_bias):
    x = np.ascontiguousarray(np.asarray(L0_attn_x, dtype=np.float32))
    W = np.asarray(L0_attn_W, dtype=np.float32)
    wfr = np.ascontiguousarray(np.asarray(L0_ffn_w_freq_real, dtype=np.float32))
    wfi = np.ascontiguousarray(np.asarray(L0_ffn_w_freq_imag, dtype=np.float32))
    bias = np.ascontiguousarray(np.asarray(L0_ffn_bias, dtype=np.float32))
    WTh = np.ascontiguousarray(W.T)

    consts = _host_consts()
    nc = build_nc()
    in_maps = []
    for core in range(8):
        b, lane = core // 4, core % 4
        xb = x[b]
        perm = perm_of(lane)
        xb_perm = np.ascontiguousarray(
            xb.reshape(16, 1024, D)[perm].reshape(16 * 1024, D))
        xown = np.stack([xb[chunk_of(lane, s) * 1024:(chunk_of(lane, s) + 1) * 1024] for s in range(NSLOT)])
        m = {
            "x_all": xb_perm,
            "x_own": np.ascontiguousarray(xown),
            "WTh": WTh,
            "wfr": wfr,
            "wfi": wfi,
            "bias_h": bias,
            "btab_h": _btable(lane, perm),
        }
        m.update(consts)
        in_maps.append(m)

    res = run_bass_kernel_spmd(nc, in_maps, core_ids=list(range(8)))
    out = np.empty((2, 16384, D), dtype=np.float32)
    for core in range(8):
        b, lane = core // 4, core % 4
        y = res.results[core]["y_own"]
        for s in range(NSLOT):
            c = chunk_of(lane, s)
            out[b, c * 1024:(c + 1) * 1024] = y[s]
    return out


# version 14
# speedup vs baseline: 1.0223x; 1.0113x over previous
import numpy as np
import ml_dtypes
import contextlib
import concourse.bass as bass
import concourse.mybir as mybir
from concourse.bass_utils import run_bass_kernel_spmd

F32 = mybir.dt.float32
BF16 = mybir.dt.bfloat16
AF = mybir.ActivationFunctionType
ALU = mybir.AluOpType
AX = mybir.AxisListType

D = 1024
NCH = 16
NSLOT = 4
LTRIP = (15, 11, 7, 3)
POS_OWN = (15, 11, 7, 3)
SLOT_OFF = (0, 15, 26, 33)
NITER = 36
NEG_BIG = -30000.0
EPS = 1e-5
NBLK = NCH
PORDER = [15, 11, 7, 3] + [p for p in range(16) if p not in (15, 11, 7, 3)]
POS2SLOT = {POS_OWN[s]: s for s in range(NSLOT)}

ENGS = ("sync", "gpsimd", "tensor", "vector", "scalar")
SEMS = ("lda", "ldf", "lds", "ldb", "ldw", "ldg", "ldg2", "stq", "stq2",
        "ldh0", "ldh1", "sst0", "sst1", "sxn2", "lck0", "lck1", "lck1g",
        "lck1s", "lcx0", "lcx1", "lxn0", "lxn1", "lxf0", "lxf1", "lxf2",
        "lxf3", "sy0", "sy1", "pe", "act", "dve")


def chunk_of(lane, s):
    return 4 * (3 - s) + ((lane + s) % 4)


def perm_of(lane):
    own = {POS_OWN[s]: chunk_of(lane, s) for s in range(NSLOT)}
    rest = sorted(set(range(16)) - set(own.values()))
    perm = []
    ri = 0
    for p in range(16):
        if p in own:
            perm.append(own[p])
        else:
            perm.append(rest[ri])
            ri += 1
    for s in range(NSLOT):
        c = chunk_of(lane, s)
        assert set(range(c)) <= set(perm[:LTRIP[s]]), (lane, s, perm)
    return perm


class Plan:
    def __init__(self, cnt):
        self.ops = {e: [] for e in ENGS}
        self.c = cnt

    def val(self, s):
        return self.c.get(s, 0)

    def add(self, eng, fn, waits=(), inc=None, by=1):
        v = None
        if inc is not None:
            v = self.c.get(inc, 0) + by
            self.c[inc] = v
        waits = tuple((s, w) for (s, w) in waits if w and w > 0)

        def run(E, S, fn=fn, waits=waits, inc=inc, by=by):
            for s, w in waits:
                E.wait_ge(S[s], w)
            ins = fn(E, S)
            if inc is not None:
                assert ins is not None
                ins.then_inc(S[inc], by)
        self.ops[eng].append(run)
        return v


def build_nc():
    nc = bass.Bass("TRN2", target_bir_lowering=False, detect_race_conditions=False)

    x_all = nc.dram_tensor("x_all", [NCH * 1024, D], F32, kind="ExternalInput").ap()
    x_own = nc.dram_tensor("x_own", [NSLOT, 1024, D], F32, kind="ExternalInput").ap()
    WTh = nc.dram_tensor("WTh", [D, 2 * D], F32, kind="ExternalInput").ap()
    wfr = nc.dram_tensor("wfr", [D, D], F32, kind="ExternalInput").ap()
    wfi = nc.dram_tensor("wfi", [D, D], F32, kind="ExternalInput").ap()
    bias_h = nc.dram_tensor("bias_h", [D], F32, kind="ExternalInput").ap()
    Fr_h = nc.dram_tensor("Fr_h", [D, D], BF16, kind="ExternalInput").ap()
    Fi_h = nc.dram_tensor("Fi_h", [D, D], BF16, kind="ExternalInput").ap()
    Gr_h = nc.dram_tensor("Gr_h", [D, D], BF16, kind="ExternalInput").ap()
    Gin_h = nc.dram_tensor("Gin_h", [D, D], BF16, kind="ExternalInput").ap()
    ident_h = nc.dram_tensor("ident_h", [128, 128], BF16, kind="ExternalInput").ap()
    trimask_h = nc.dram_tensor("trimask_h", [128, 128], BF16, kind="ExternalInput").ap()
    btab_h = nc.dram_tensor("btab_h", [128, NITER], F32, kind="ExternalInput").ap()
    y_own = nc.dram_tensor("y_own", [NSLOT, 1024, D], F32, kind="ExternalOutput").ap()

    KT_scr = nc.dram_tensor("KT_scr", [D, NBLK * 1024], BF16, kind="Internal").ap()
    V_scr = nc.dram_tensor("V_scr", [NBLK * 1024, D], BF16, kind="Internal").ap()
    xnT_scr = nc.dram_tensor("xnT_scr", [NSLOT, 8, 128, 1024], BF16, kind="Internal").ap()
    xn2_scr = nc.dram_tensor("xn2_scr", [NSLOT, 8, 128, D], BF16, kind="Internal").ap()

    xall_t = x_all.rearrange("(n p) d -> p n d", p=128)
    xown_t = x_own.rearrange("s (n p) d -> p s n d", p=128)
    ktscr_t = KT_scr.rearrange("(a p) t -> p a t", p=128)
    vscr_t = V_scr.rearrange("(n p) d -> p n d", p=128)
    xnscr_t = xnT_scr.rearrange("s c p t -> p s c t")
    xn2scr_t = xn2_scr.rearrange("s q p d -> p s q d")
    yown_t = y_own.rearrange("s (n p) d -> p s n d", p=128)

    cnt = {}
    with contextlib.ExitStack() as st:
        S = {}
        for name in SEMS:
            S[name] = st.enter_context(nc.semaphore(name))

        def dispatch(plan):
            with nc.Block() as blk:
                for e in ENGS:
                    ops = plan.ops[e]
                    if not ops:
                        continue

                    def body(E, ops=ops):
                        for f in ops:
                            f(E, S)
                    getattr(blk, e)(body)

        M_sb = st.enter_context(nc.sbuf_tensor("M_sb", [128, 8, D], BF16))
        bias_bc = st.enter_context(nc.sbuf_tensor("bias_bc", [128, D], F32))
        ident = st.enter_context(nc.sbuf_tensor("ident", [128, 128], BF16))
        trimask = st.enter_context(nc.sbuf_tensor("trimask", [128, 128], BF16))
        btab = st.enter_context(nc.sbuf_tensor("btab", [128, NITER], F32))
        cstk = st.enter_context(nc.sbuf_tensor("cstk", [128, 4], F32))

        with contextlib.ExitStack() as stAB:
            WT_sb = stAB.enter_context(nc.sbuf_tensor("WT_sb", [128, 8, 2 * D], BF16))
            xstB = stAB.enter_context(nc.sbuf_tensor("xstB", [128, 2, 4, D], F32))
            xn_bf = stAB.enter_context(nc.sbuf_tensor("xn_bf", [128, 8, D], BF16))
            bns = stAB.enter_context(nc.sbuf_tensor("bns", [128, 8, 12], F32))
            mv2 = stAB.enter_context(nc.sbuf_tensor("mv2", [128, 8, 2], F32))
            sv = stAB.enter_context(nc.sbuf_tensor("sv", [128, 8], F32))
            r8 = stAB.enter_context(nc.sbuf_tensor("r8", [128, 8], F32))

            v_xn = {}
            v_evxnT = {}
            v_store = {}
            v_xn2st = {}
            vevK = {}
            vevV = {}
            v_half = {}
            v_trB = {}

            def emit_ln_stats(plan, idx):
                vst2 = {}
                for q in range(8):
                    hid = 2 * idx + q // 4
                    hp = hid % 2
                    xsrc = xstB[:, hp, q % 4, :]
                    w = [("ldh%d" % hp, v_half[hid])] if q % 4 == 0 else []
                    plan.add("vector", lambda E, S, xsrc=xsrc, q=q: E.bn_stats(bns[:, q, 0:6], xsrc[:, 0:512]), waits=w)
                    vst2[q] = plan.add("vector", lambda E, S, xsrc=xsrc, q=q: E.bn_stats(bns[:, q, 6:12], xsrc[:, 512:1024]), inc="dve")
                va = None
                for q in range(8):
                    va = plan.add("vector", lambda E, S, q=q: E.bn_aggr(mv2[:, q, :], bns[:, q, :]),
                                  waits=[("dve", vst2[q])], inc="dve")
                return va

            def emit_ln_fin(plan, va):
                vsq = plan.add("scalar", lambda E, S: E.activation(sv[:, 0:8], mv2[:, :, 1:2].rearrange("p q o -> p (q o)"),
                                                                  AF.Sqrt, bias=cstk[:, 0:1]),
                               waits=[("dve", va)], inc="act")
                vrc = plan.add("vector", lambda E, S: E.reciprocal(r8[:, 0:8], sv[:, 0:8]), waits=[("act", vsq)], inc="dve")
                return vrc

            def emit_ln_ts(plan, idx, vrc):
                for q in range(8):
                    ti = idx * 8 + q
                    hid = 2 * idx + q // 4
                    hp = hid % 2
                    xsrc = xstB[:, hp, q % 4, :]
                    w = [("dve", vrc), ("pe", v_trB.get(ti - 8, 0)), ("sxn2", v_xn2st.get(idx - 1, 0))]
                    v_xn[ti] = plan.add("vector", lambda E, S, xsrc=xsrc, q=q: E.tensor_scalar(
                        xn_bf[:, q, :], xsrc, mv2[:, q, 0:1], r8[:, q:q + 1], ALU.subtract, ALU.mult),
                        waits=w, inc="dve")

            pA = Plan(cnt)
            with contextlib.ExitStack() as stA:
                ArT = stA.enter_context(nc.sbuf_tensor("ArT", [128, 8, D], BF16))
                AiT = stA.enter_context(nc.sbuf_tensor("AiT", [128, 8, D], BF16))
                psA0 = stA.enter_context(nc.psum_tensor("psA0", [128, D], F32))
                psA1 = stA.enter_context(nc.psum_tensor("psA1", [128, D], F32))
                psA = (psA0, psA1)

                pos0 = PORDER[0]
                for h in range(2):
                    src_ = xall_t[:, pos0 * 8 + h * 4:pos0 * 8 + h * 4 + 4, :]
                    v_half[h] = pA.add("scalar", lambda E, S, src_=src_, h=h: E.dma_start(out=xstB[:, h, :, :], in_=src_),
                                       inc="ldh%d" % h, by=16)

                pA.add("vector", lambda E, S: E.memset(cstk[:, 0:1], EPS))
                pA.add("vector", lambda E, S: E.memset(cstk[:, 1:2], -0.5))
                pA.add("vector", lambda E, S: E.memset(cstk[:, 2:3], -1.0), inc="dve")
                va0 = emit_ln_stats(pA, 0)

                with contextlib.ExitStack() as stA1:
                    Fr_sb = stA1.enter_context(nc.sbuf_tensor("Fr_sb", [128, 8, D], BF16))
                    Fi_sb = stA1.enter_context(nc.sbuf_tensor("Fi_sb", [128, 8, D], BF16))
                    wr_bf = stA1.enter_context(nc.sbuf_tensor("wr_bf", [128, 8, D], BF16))
                    wi_bf = stA1.enter_context(nc.sbuf_tensor("wi_bf", [128, 8, D], BF16))

                    vFr = pA.add("sync", lambda E, S: E.dma_start(out=Fr_sb[:], in_=Fr_h.rearrange("(c p) t -> p c t", p=128)), inc="ldf", by=16)
                    pA.add("sync", lambda E, S: E.dma_start(out=ident[:], in_=ident_h), inc="lda", by=16)
                    pA.add("sync", lambda E, S: E.dma_start(out=trimask[:], in_=trimask_h), inc="lda", by=16)
                    vlda = pA.add("sync", lambda E, S: E.dma_start(out=btab[:], in_=btab_h), inc="lda", by=16)
                    vFi = pA.add("scalar", lambda E, S: E.dma_start(out=Fi_sb[:], in_=Fi_h.rearrange("(c p) t -> p c t", p=128)), inc="lds", by=16)
                    vwi = pA.add("gpsimd", lambda E, S: E.dma_start(out=wi_bf[:], in_=wfi.rearrange("(c p) o -> p c o", p=128)), inc="stq", by=16)
                    vwr = pA.add("gpsimd", lambda E, S: E.dma_start(out=wr_bf[:], in_=wfr.rearrange("(c p) o -> p c o", p=128)), inc="stq2", by=16)
                    vWT = pA.add("gpsimd", lambda E, S: E.dma_start(out=WT_sb[:], in_=WTh.rearrange("(c p) f -> p c f", p=128)),
                                 inc="ldw", by=16)

                    evA = []
                    vneg = None
                    vmmlast = None
                    for grp in range(2):
                        if grp == 0:
                            dstT, l0, l1 = AiT, wi_bf, wr_bf
                            w_a = [("stq", vwi), ("ldf", vFr)]
                            w_b = [("stq2", vwr), ("lds", vFi)]
                        else:
                            dstT, l0, l1 = ArT, wr_bf, wi_bf
                            w_a = [("stq2", vwr), ("ldf", vFr)]
                            w_b = [("dve", vneg), ("lds", vFi)]
                        for ot in range(8):
                            idx = grp * 8 + ot
                            ps = psA[idx % 2]
                            wa = list(w_a)
                            if idx >= 2:
                                wa.append(("act", evA[idx - 2]))

                            def mm_p1(E, S, ps=ps, l0=l0, ot=ot):
                                last = None
                                for h in range(2):
                                    for d in range(8):
                                        last = E.matmul(
                                            ps[:, h * 512:(h + 1) * 512],
                                            l0[:, d, ot * 128:(ot + 1) * 128],
                                            Fr_sb[:, d, h * 512:(h + 1) * 512],
                                            start=(d == 0), stop=False)
                                return last
                            pA.add("tensor", mm_p1, waits=wa, inc="pe")

                            def mm_p2(E, S, ps=ps, l1=l1, ot=ot):
                                last = None
                                for h in range(2):
                                    for d in range(8):
                                        last = E.matmul(
                                            ps[:, h * 512:(h + 1) * 512],
                                            l1[:, d, ot * 128:(ot + 1) * 128],
                                            Fi_sb[:, d, h * 512:(h + 1) * 512],
                                            start=False, stop=(d == 7))
                                return last
                            vmmlast = pA.add("tensor", mm_p2, waits=list(w_b), inc="pe")
                            evA.append(pA.add("scalar", lambda E, S, ps=ps, dstT=dstT, ot=ot: E.copy(dstT[:, ot, :], ps[:]),
                                              waits=[("pe", vmmlast)], inc="act"))
                        if grp == 0:
                            vneg = pA.add("vector", lambda E, S: E.tensor_scalar_mul(wi_bf[:], wi_bf[:], -1.0),
                                          waits=[("pe", vmmlast)], inc="dve")
                            vrc0 = emit_ln_fin(pA, va0)
                    vA_ev_done = evA[-1]
                    emit_ln_ts(pA, 0, vrc0)

                with contextlib.ExitStack() as stA2:
                    Gr_sb = stA2.enter_context(nc.sbuf_tensor("Gr_sb", [128, 8, D], BF16))
                    Gin_sb = stA2.enter_context(nc.sbuf_tensor("Gin_sb", [128, 8, D], BF16))
                    vGr = pA.add("sync", lambda E, S: E.dma_start(out=Gr_sb[:], in_=Gr_h.rearrange("(c p) t -> p c t", p=128)),
                                 waits=[("pe", vmmlast)], inc="ldg", by=16)
                    vGin = pA.add("sync", lambda E, S: E.dma_start(out=Gin_sb[:], in_=Gin_h.rearrange("(c p) t -> p c t", p=128)),
                                  inc="ldg2", by=16)

                    evM = []
                    for tt in range(8):
                        ps = psA[tt % 2]
                        wa = [("ldg", vGr), ("act", vA_ev_done)]
                        if tt >= 2:
                            wa.append(("act", evM[tt - 2]))

                        def mm_a(E, S, ps=ps, tt=tt):
                            last = None
                            for h in range(2):
                                for k in range(8):
                                    last = E.matmul(ps[:, h * 512:(h + 1) * 512],
                                                    ArT[:, k, tt * 128:(tt + 1) * 128],
                                                    Gr_sb[:, k, h * 512:(h + 1) * 512],
                                                    start=(k == 0), stop=False)
                            return last
                        pA.add("tensor", mm_a, waits=wa, inc="pe")

                        def mm_b(E, S, ps=ps, tt=tt):
                            last = None
                            for h in range(2):
                                for k in range(8):
                                    last = E.matmul(ps[:, h * 512:(h + 1) * 512],
                                                    AiT[:, k, tt * 128:(tt + 1) * 128],
                                                    Gin_sb[:, k, h * 512:(h + 1) * 512],
                                                    start=False, stop=(k == 7))
                            return last
                        vmm = pA.add("tensor", mm_b, waits=[("ldg2", vGin)], inc="pe")
                        evM.append(pA.add("scalar", lambda E, S, ps=ps, tt=tt: E.copy(M_sb[:, tt, :], ps[:]),
                                          waits=[("pe", vmm)], inc="act"))
                vA_act = pA.val("act")
                vA_pe = pA.val("pe")
                vA_dve = pA.val("dve")
                dispatch(pA)

            pB = Plan(cnt)
            barrier = (("act", vA_act), ("pe", vA_pe), ("dve", vA_dve))
            with contextlib.ExitStack() as stB:
                with contextlib.ExitStack() as stB0:
                    biasrow = stB0.enter_context(nc.sbuf_tensor("biasrow", [1, D], F32))
                    ones1f = stB0.enter_context(nc.sbuf_tensor("ones1f", [1, 128], F32))
                    psB0 = stB0.enter_context(nc.psum_tensor("psB0", [128, D], F32))

                    vbr = pB.add("sync", lambda E, S: E.dma_start(out=biasrow[:], in_=bias_h[None, :]), waits=barrier, inc="ldb", by=16)
                    vones = pB.add("vector", lambda E, S: E.memset(ones1f[:], 1.0), waits=barrier, inc="dve")

                    def mmbias(E, S):
                        E.matmul(psB0[:, 0:512], ones1f[0:1, :], biasrow[0:1, 0:512], start=True, stop=True)
                        return E.matmul(psB0[:, 512:1024], ones1f[0:1, :], biasrow[0:1, 512:1024], start=True, stop=True)
                    vbb = pB.add("tensor", mmbias, waits=barrier + (("ldb", vbr), ("dve", vones)), inc="pe")
                    pB.add("vector", lambda E, S: E.tensor_copy(bias_bc[:], psB0[:]), waits=[("pe", vbb)], inc="dve")
                    vB0_dve = pB.val("dve")
                    vB0_pe = pB.val("pe")

                xnT_B = stB.enter_context(nc.sbuf_tensor("xnT_B", [128, 2, 8, D], BF16))
                KT_blk = stB.enter_context(nc.sbuf_tensor("KT_blk", [128, 2, 8, D], BF16))
                V_blk = stB.enter_context(nc.sbuf_tensor("V_blk", [128, 2, 8, D], BF16))
                psK0 = stB.enter_context(nc.psum_tensor("psK0", [128, D], F32))
                psK1 = stB.enter_context(nc.psum_tensor("psK1", [128, D], F32))
                psV0 = stB.enter_context(nc.psum_tensor("psV0", [128, D], F32))
                psV1 = stB.enter_context(nc.psum_tensor("psV1", [128, D], F32))
                psK = (psK0, psK1)
                psV = (psV0, psV1)

                def emit_halves(idx):
                    pos = PORDER[idx]
                    for h in range(2):
                        hid = 2 * idx + h
                        hp = hid % 2
                        src_ = xall_t[:, pos * 8 + h * 4:pos * 8 + h * 4 + 4, :]
                        w = [("dve", v_xn.get((hid - 2) * 4 + 3, 0))]
                        v_half[hid] = pB.add("sync", lambda E, S, src_=src_, hp=hp: E.dma_start(out=xstB[:, hp, :, :], in_=src_),
                                             waits=w, inc="ldh%d" % hp, by=16)

                def emit_xn2(idx):
                    s_ = POS2SLOT[PORDER[idx]]
                    v_xn2st[idx] = pB.add("gpsimd", lambda E, S, s_=s_: E.dma_start(out=xn2scr_t[:, s_, :, :], in_=xn_bf[:]),
                                          waits=[("dve", v_xn[idx * 8 + 7])], inc="sxn2", by=16)

                def emit_tr(idx):
                    jp = idx % 2
                    regs = (psK[0], psK[1], psV[0], psV[1])
                    for q in range(8):
                        ti = idx * 8 + q
                        r4 = q % 4
                        wtr = [("dve", v_xn[ti]), ("lda", vlda)]
                        if r4 < 2:
                            wtr.append(("act", vevK.get((idx - 1, 6 + r4), 0)))
                        else:
                            wtr.append(("act", vevV.get((idx - 1, 4 + r4), 0)))
                        pv = v_evxnT.get(ti - 4)
                        if pv is not None:
                            wtr.append((pv[0], pv[1]))
                        if ti == 0:
                            wtr += [("pe", vB0_pe), ("dve", vB0_dve)]

                        def trB(E, S, q=q, r4=r4, regs=regs):
                            last = None
                            for d in range(8):
                                last = E.matmul(regs[r4][:, d * 128:(d + 1) * 128], xn_bf[:, q, d * 128:(d + 1) * 128],
                                                ident[:], start=True, stop=True)
                            return last
                        vtr = pB.add("tensor", trB, waits=wtr, inc="pe")
                        v_trB[ti] = vtr
                        wev = [("pe", vtr)]
                        if q == 0:
                            wev.append(("sst%d" % jp, v_store.get(idx - 2, 0)))

                        def evxnT_v(E, S, r4=r4, jp=jp, q=q, regs=regs):
                            return E.tensor_copy(xnT_B[:, jp, :, q * 128:(q + 1) * 128], regs[r4].rearrange("p (c t) -> p c t", c=8))

                        def evxnT_s(E, S, r4=r4, jp=jp, q=q, regs=regs):
                            return E.copy(xnT_B[:, jp, :, q * 128:(q + 1) * 128], regs[r4].rearrange("p (c t) -> p c t", c=8))
                        if q % 2 == 0:
                            v_evxnT[ti] = ("dve", pB.add("vector", evxnT_v, waits=wev, inc="dve"))
                        else:
                            v_evxnT[ti] = ("act", pB.add("scalar", evxnT_s, waits=wev, inc="act"))

                def _evw(idx, qmax):
                    wd = wa = 0
                    for q in range(qmax + 1):
                        e, v = v_evxnT[idx * 8 + q]
                        if e == "dve":
                            wd = max(wd, v)
                        else:
                            wa = max(wa, v)
                    return [("dve", wd), ("act", wa)]

                def emit_kv(idx, f0, f1):
                    jp = idx % 2
                    for ftile in range(f0, f1):
                        for kind in range(2):
                            kp = ftile % 2
                            for h in range(2):
                                wmm = _evw(idx, 3 if h == 0 else 7)
                                if idx == 0 and ftile == 0 and h == 0:
                                    wmm.append(("ldw", vWT))
                                if h == 0:
                                    if kind == 0:
                                        if ftile >= 2:
                                            wmm.append(("act", vevK.get((idx, ftile - 2), 0)))
                                        else:
                                            e, v = v_evxnT[idx * 8 + 4 + kp]
                                            wmm.append((e, v))
                                    else:
                                        if ftile >= 2:
                                            wmm.append(("act", vevV.get((idx, ftile - 2), 0)))
                                        else:
                                            e, v = v_evxnT[idx * 8 + 6 + kp]
                                            wmm.append((e, v))
                                if kind == 0:
                                    def mmK(E, S, jp=jp, ft=ftile, kp=kp, h=h):
                                        last = None
                                        for d in range(8):
                                            last = E.matmul(psK[kp][:, h * 512:(h + 1) * 512],
                                                            WT_sb[:, d, ft * 128:(ft + 1) * 128],
                                                            xnT_B[:, jp, d, h * 512:(h + 1) * 512],
                                                            start=(d == 0), stop=(d == 7))
                                        return last
                                    vmm = pB.add("tensor", mmK, waits=wmm, inc="pe")
                                else:
                                    def mmV(E, S, jp=jp, tt=ftile, kp=kp, h=h):
                                        last = None
                                        for d in range(8):
                                            last = E.matmul(psV[kp][:, h * 512:(h + 1) * 512],
                                                            xnT_B[:, jp, d, tt * 128:(tt + 1) * 128],
                                                            WT_sb[:, d, D + h * 512:D + (h + 1) * 512],
                                                            start=(d == 0), stop=(d == 7))
                                        return last
                                    vmm = pB.add("tensor", mmV, waits=wmm, inc="pe")
                            wev = [("pe", vmm)]
                            if ftile < 2:
                                wev.append(("sst%d" % jp, v_store.get(idx - 2, 0)))
                            if kind == 0:
                                vevK[(idx, ftile)] = pB.add("scalar", lambda E, S, jp=jp, ft=ftile, kp=kp: E.copy(KT_blk[:, jp, ft, :], psK[kp][:]),
                                                            waits=wev, inc="act")
                            else:
                                vevV[(idx, ftile)] = pB.add("scalar", lambda E, S, jp=jp, tt=ftile, kp=kp: E.copy(V_blk[:, jp, tt, :], psV[kp][:]),
                                                            waits=wev, inc="act")

                def emit_stores(idx):
                    jp = idx % 2
                    pos = PORDER[idx]
                    sem = "sst%d" % jp
                    pB.add("gpsimd", lambda E, S, jp=jp, pos=pos: E.dma_start(out=ktscr_t[:, :, pos * 1024:(pos + 1) * 1024], in_=KT_blk[:, jp, :, :]),
                           waits=[("act", vevK[(idx, 7)])], inc=sem, by=16)
                    vst = pB.add("gpsimd", lambda E, S, jp=jp, pos=pos: E.dma_start(out=vscr_t[:, pos * 8:(pos + 1) * 8, :], in_=V_blk[:, jp, :, :]),
                                 waits=[("act", vevV[(idx, 7)])], inc=sem, by=16)
                    if pos in POS2SLOT:
                        s_ = POS2SLOT[pos]
                        e, v = v_evxnT[idx * 8 + 7]
                        vst = pB.add("gpsimd", lambda E, S, jp=jp, s_=s_: E.dma_start(out=xnscr_t[:, s_, :, :], in_=xnT_B[:, jp, :, :]),
                                     waits=[(e, v), ("dve", v_evxnT[idx * 8 + 6][1])], inc=sem, by=16)
                    v_store[idx] = vst

                for idx in range(NBLK):
                    if PORDER[idx] in POS2SLOT:
                        emit_xn2(idx)
                    emit_tr(idx)
                    emit_kv(idx, 0, 4)
                    if idx + 1 < NBLK:
                        emit_halves(idx + 1)
                        va = emit_ln_stats(pB, idx + 1)
                        vrc = emit_ln_fin(pB, va)
                    emit_kv(idx, 4, 8)
                    if idx + 1 < NBLK:
                        emit_ln_ts(pB, idx + 1, vrc)
                    emit_stores(idx)

                vB_act = pB.val("act")
                vB_pe = pB.val("pe")
                vB_dve = pB.val("dve")
                vB_sst0 = pB.val("sst0")
                vB_sst1 = pB.val("sst1")
                vB_sxn2 = pB.val("sxn2")
                dispatch(pB)

        pC = Plan(cnt)
        with contextlib.ExitStack() as stC:
            KTs = stC.enter_context(nc.sbuf_tensor("KTs", [128, 2, 8, D], BF16))
            Vs = stC.enter_context(nc.sbuf_tensor("Vs", [128, 2, 8, D], BF16))
            acc = stC.enter_context(nc.sbuf_tensor("acc", [128, 8, D], F32))
            P_sb = stC.enter_context(nc.sbuf_tensor("P_sb", [128, 2, D], BF16))
            PT_sb = stC.enter_context(nc.sbuf_tensor("PT_sb", [128, D], BF16))
            xnT_own = stC.enter_context(nc.sbuf_tensor("xnT_own", [128, 2, 8, D], BF16))
            xn_ld = stC.enter_context(nc.sbuf_tensor("xn_ld", [128, 2, D], BF16))
            xstC = stC.enter_context(nc.sbuf_tensor("xstC", [128, 4, D], F32))
            s_sb = stC.enter_context(nc.sbuf_tensor("s_sb", [128, 2, D], F32))
            yb_sb = stC.enter_context(nc.sbuf_tensor("yb_sb", [128, 2, D], F32))
            gsil = stC.enter_context(nc.sbuf_tensor("gsil", [128, 2, D], F32))
            negm = stC.enter_context(nc.sbuf_tensor("negm", [128, 2, 8], F32))
            lsm = stC.enter_context(nc.sbuf_tensor("lsm", [128, 2, 8], F32))
            bP = stC.enter_context(nc.sbuf_tensor("bP", [128, 2, 8], F32))
            rr = stC.enter_context(nc.sbuf_tensor("rr", [128, 2, 8], F32))
            tmpm = stC.enter_context(nc.sbuf_tensor("tmpm", [128, 8], F32))
            lpart = stC.enter_context(nc.sbuf_tensor("lpart", [128, 8], F32))
            invl = stC.enter_context(nc.sbuf_tensor("invl", [128, 8], F32))
            psS0 = stC.enter_context(nc.psum_tensor("psS0", [128, D], F32))
            psS1 = stC.enter_context(nc.psum_tensor("psS1", [128, D], F32))
            psT = stC.enter_context(nc.psum_tensor("psT", [128, D], F32))
            psVC = stC.enter_context(nc.psum_tensor("psVC", [128, D], F32))
            psS = (psS0, psS1)

            barrierB = (("act", vB_act), ("pe", vB_pe), ("dve", vB_dve))
            stqB = (("sst0", vB_sst0), ("sst1", vB_sst1), ("sxn2", vB_sxn2))
            last_rd_buf = {0: 0, 1: 0}
            fin_pe_bypar = {0: 0, 1: 0}
            prev_ex_bybank = {0: 0, 1: 0}
            prev_tr_bypar = {0: 0, 1: 0}
            prev_evt_last = 0
            prev_psVC_free = 0
            prev_psT_free = 0
            prev_fin_dma = {0: 0, 1: 0}
            prev_out4 = {0: 0, 1: 0, 2: 0, 3: 0}
            prev_Sfree_bypar = {0: 0, 1: 0}

            def emit_prep(s):
                sp = s % 2
                dblk = POS_OWN[s]
                xq = "scalar" if s == 0 else "sync"
                kq, ksem = ("scalar", "lck1s") if s == 0 else ("gpsimd", "lck1g")
                wp = [("pe", fin_pe_bypar[sp])]
                if s == 0:
                    wp += list(barrierB) + list(stqB)
                v_xnl = pC.add(xq, lambda E, S, s=s, sp=sp: E.dma_start(out=xnT_own[:, sp, :, :], in_=xnscr_t[:, s, :, :]),
                               waits=wp, inc="lcx%d" % sp, by=16)
                wk = [("pe", last_rd_buf[0])]
                if s == 0:
                    wk += list(barrierB) + list(stqB)
                pC.add("sync", lambda E, S, dblk=dblk: E.dma_start(out=KTs[:, 0, :, :], in_=ktscr_t[:, :, dblk * 1024:(dblk + 1) * 1024]),
                       waits=wk, inc="lck0", by=16)
                v_vl = pC.add("sync", lambda E, S, dblk=dblk: E.dma_start(out=Vs[:, 0, :, :], in_=vscr_t[:, dblk * 8:(dblk + 1) * 8, :]),
                              inc="lck0", by=16)
                wk1 = [("pe", last_rd_buf[1])]
                if s == 0:
                    wk1 += list(barrierB) + list(stqB)
                pC.add(kq, lambda E, S: E.dma_start(out=KTs[:, 1, :, :], in_=ktscr_t[:, :, 0:1024]),
                       waits=wk1, inc=ksem, by=16)
                v_v1 = pC.add(kq, lambda E, S: E.dma_start(out=Vs[:, 1, :, :], in_=vscr_t[:, 0:8, :]),
                              inc=ksem, by=16)
                return {"xnl": ("lcx%d" % sp, v_xnl), "pair": {0: ("lck0", v_vl), 1: (ksem, v_v1)}}

            pending = emit_prep(0)

            for s in range(NSLOT):
                L = LTRIP[s]
                sp = s % 2
                v_xnl = pending["xnl"]
                v_ldpair = dict(pending["pair"])

                def emit_load(j):
                    bp = (j + 1) % 2
                    sem = "lck%d" % bp
                    pC.add("sync", lambda E, S, bp=bp, j=j: E.dma_start(out=KTs[:, bp, :, :], in_=ktscr_t[:, :, j * 1024:(j + 1) * 1024]),
                           waits=[("pe", last_rd_buf[bp])], inc=sem, by=16)
                    vv = pC.add("sync", lambda E, S, bp=bp, j=j: E.dma_start(out=Vs[:, bp, :, :], in_=vscr_t[:, j * 8:(j + 1) * 8, :]),
                               inc=sem, by=16)
                    v_ldpair[j + 1] = (sem, vv)

                steps = [("diag", 0, t) for t in range(8)]
                for j in range(L):
                    steps += [("stream", j + 1, t) for t in range(8)]
                K = len(steps)

                vSC = {}
                vTR = {}
                vPV = {}
                vST = {}
                vUP = {}
                vEX = {}
                vR = {}
                vEVT = {}

                def bufp_of(i):
                    return 0 if i == 0 else i % 2

                def emit_SC(k):
                    kind, i, t = steps[k]
                    bp = bufp_of(i)
                    ps = psS[k % 2]
                    w = []
                    if k >= 2:
                        w.append(("act", vEX[k - 2]))
                    else:
                        w.append(("act", prev_ex_bybank[k % 2]))
                        w.append(v_xnl)
                        w.append(v_ldpair[0])
                        w.append(("lda", vlda))
                    if t == 0 and i > 0:
                        w.append(v_ldpair[i])
                    if kind == "diag":
                        ncols = (t + 1) * 128

                        def fn(E, S, t=t, bp=bp, ncols=ncols, sp=sp, ps=ps):
                            last = None
                            tri_lo = ncols - 128
                            for h in range(2):
                                lo = h * 512
                                hi = min(ncols, (h + 1) * 512)
                                if hi <= lo:
                                    break
                                has_tri = (tri_lo >= lo) and (ncols <= (h + 1) * 512)
                                for d in range(8):
                                    last = E.matmul(ps[:, lo:hi], xnT_own[:, sp, d, t * 128:(t + 1) * 128],
                                                    KTs[:, bp, d, lo:hi], start=(d == 0),
                                                    stop=(d == 7 and not has_tri), skip_group_check=True)
                                if has_tri:
                                    last = E.matmul(ps[:, tri_lo:ncols], ident[:], trimask[:],
                                                    start=False, stop=True, skip_group_check=True)
                            return last
                    else:
                        def fn(E, S, t=t, bp=bp, sp=sp, ps=ps):
                            last = None
                            for h in range(2):
                                for d in range(8):
                                    last = E.matmul(ps[:, h * 512:(h + 1) * 512], xnT_own[:, sp, d, t * 128:(t + 1) * 128],
                                                    KTs[:, bp, d, h * 512:(h + 1) * 512], start=(d == 0), stop=(d == 7))
                            return last
                    vSC[k] = pC.add("tensor", fn, waits=w, inc="pe")

                def emit_stats(k):
                    kind, i, t = steps[k]
                    p, q = i % 2, 1 - i % 2
                    ps = psS[k % 2]
                    w = [("pe", vSC[k])]
                    if kind == "diag":
                        ncols = (t + 1) * 128
                        vST[k] = pC.add("vector", lambda E, S, p=p, t=t, ncols=ncols, ps=ps: E.reduce_max(negm[:, p, t:t + 1], ps[:, 0:ncols], axis=AX.X, negate=True),
                                        waits=w, inc="dve")
                    else:
                        itcol = SLOT_OFF[s] + (i - 1)
                        v1 = pC.add("vector", lambda E, S, t=t, ps=ps: E.reduce_max(tmpm[:, t:t + 1], ps[:], axis=AX.X, negate=True),
                                    waits=w, inc="dve")
                        v2 = pC.add("vector", lambda E, S, p=p, q=q, t=t, itcol=itcol: E.tensor_scalar(negm[:, p, t:t + 1], tmpm[:, t:t + 1], btab[:, itcol:itcol + 1], negm[:, q, t:t + 1], ALU.add, ALU.min),
                                    waits=[("dve", v1)], inc="dve")
                        vST[k] = pC.add("vector", lambda E, S, p=p, t=t, itcol=itcol: E.tensor_scalar_sub(bP[:, p, t:t + 1], negm[:, p, t:t + 1], btab[:, itcol:itcol + 1]),
                                        waits=[("dve", v2)], inc="dve")

                def emit_R_EX(k):
                    kind, i, t = steps[k]
                    p, q = i % 2, 1 - i % 2
                    pp = k % 2
                    ps = psS[pp]
                    ncols = (t + 1) * 128 if kind == "diag" else 1024
                    if kind == "stream":
                        vR[k] = pC.add("scalar", lambda E, S, p=p, q=q, t=t: E.activation(rr[:, p, t:t + 1], negm[:, q, t:t + 1], AF.Exp, bias=negm[:, p, t:t + 1], scale=cstk[:, 2:3]),
                                       waits=[("dve", vST[k])], inc="act")
                        bt = bP
                    else:
                        vR[k] = None
                        bt = negm

                    def fn(E, S, bt=bt, p=p, t=t, pp=pp, ncols=ncols, ps=ps):
                        return E.activation(P_sb[:, pp, 0:ncols], ps[:, 0:ncols], AF.Exp,
                                            bias=bt[:, p, t:t + 1], accum_out=lpart[:, t:t + 1])
                    wex = [("dve", vST[k]), ("pe", vTR.get(k - 2, prev_tr_bypar[pp]))]
                    vEX[k] = pC.add("scalar", fn, waits=wex, inc="act")

                def emit_TR(k):
                    kind, i, t = steps[k]
                    nsub = (t + 1) if kind == "diag" else 8
                    pp = k % 2
                    w = [("act", vEX[k])]
                    if k == 0:
                        w.append(("act", prev_evt_last))
                        w.append(("dve", prev_psT_free))

                    def fn(E, S, nsub=nsub, pp=pp):
                        last = None
                        for jj in range(nsub):
                            last = E.matmul(psT[:, jj * 128:(jj + 1) * 128], P_sb[:, pp, jj * 128:(jj + 1) * 128],
                                            ident[:], start=True, stop=True)
                        return last
                    vTR[k] = pC.add("tensor", fn, waits=w, inc="pe")

                def emit_EVT(k):
                    kind, i, t = steps[k]
                    ncols = ((t + 1) if kind == "diag" else 8) * 128
                    vEVT[k] = pC.add("scalar", lambda E, S, ncols=ncols: E.copy(PT_sb[:, 0:ncols], psT[:, 0:ncols]),
                                     waits=[("pe", vTR[k])], inc="act")

                def emit_PV(k):
                    kind, i, t = steps[k]
                    nsub = (t + 1) if kind == "diag" else 8
                    bp = bufp_of(i)
                    w = [("act", vEVT[k]), ("dve", vUP.get(k - 1, prev_psVC_free))]
                    if t == 0 and i > 0:
                        w.append(v_ldpair[i])

                    def fn(E, S, nsub=nsub, bp=bp):
                        last = None
                        for jj in range(nsub):
                            for h in range(2):
                                last = E.matmul(psVC[:, h * 512:(h + 1) * 512], PT_sb[:, jj * 128:(jj + 1) * 128],
                                                Vs[:, bp, jj, h * 512:(h + 1) * 512], start=(jj == 0), stop=(jj == nsub - 1))
                        return last
                    vPV[k] = pC.add("tensor", fn, waits=w, inc="pe")
                    last_rd_buf[bp] = vPV[k]

                def emit_UP(k):
                    kind, i, t = steps[k]
                    p, q = i % 2, 1 - i % 2
                    w = [("pe", vPV[k])]
                    if kind == "diag":
                        pC.add("vector", lambda E, S, p=p, t=t: E.tensor_copy(lsm[:, p, t:t + 1], lpart[:, t:t + 1]), waits=w)
                        vUP[k] = pC.add("vector", lambda E, S, t=t: E.tensor_copy(acc[:, t, :], psVC[:]), inc="dve")
                    else:
                        w.append(("act", vR[k]))
                        pC.add("vector", lambda E, S, p=p, q=q, t=t: E.scalar_tensor_tensor(lsm[:, p, t:t + 1], lsm[:, q, t:t + 1], rr[:, p, t:t + 1], lpart[:, t:t + 1], ALU.mult, ALU.add),
                               waits=w)
                        vUP[k] = pC.add("vector", lambda E, S, p=p, t=t: E.scalar_tensor_tensor(acc[:, t, :], acc[:, t, :], rr[:, p, t:t + 1], psVC[:], ALU.mult, ALU.add),
                                        inc="dve")

                emit_SC(0)
                emit_stats(0)
                emit_R_EX(0)
                emit_SC(1)
                emit_stats(1)
                emit_R_EX(1)
                for k in range(K):
                    kind, i, t = steps[k]
                    emit_TR(k)
                    emit_EVT(k)
                    if k + 2 < K:
                        emit_SC(k + 2)
                        emit_stats(k + 2)
                        emit_R_EX(k + 2)
                    emit_PV(k)
                    emit_UP(k)
                    if t == 7:
                        nx = i + 2
                        if 2 <= nx <= L:
                            emit_load(nx - 1)

                if s + 1 < NSLOT:
                    pending = emit_prep(s + 1)

                lp = L % 2
                vS_ = {}
                vY = {}
                vYB = {}
                vG = {}
                vOUT = {}
                vfdma = {}
                vinvl = {}
                vxnl2 = {}
                vXF = {}
                for t in range(2):
                    vxnl2[t] = pC.add("scalar", lambda E, S, s=s, t=t: E.dma_start(out=xn_ld[:, t % 2, :], in_=xn2scr_t[:, s, t, :]),
                                      waits=[("dve", prev_Sfree_bypar[t % 2]), ("sxn2", vB_sxn2)], inc="lxn%d" % (t % 2), by=16)
                for t in range(4):
                    vXF[t] = pC.add("scalar", lambda E, S, s=s, t=t: E.dma_start(out=xstC[:, t % 4, :], in_=xown_t[:, s, t, :]),
                                    waits=[("dve", prev_out4[t % 4])], inc="lxf%d" % (t % 4), by=16)
                for t in range(8):
                    tp = t % 2
                    t4 = t % 4
                    vinvl[t] = pC.add("vector", lambda E, S, t=t, lp=lp: E.reciprocal(invl[:, t:t + 1], lsm[:, lp, t:t + 1]),
                                      waits=[("dve", vUP[K - 8 + t])], inc="dve")
                    wS = [("dve", vinvl[t]), ("lxn%d" % tp, vxnl2[t]),
                          ("sy%d" % tp, prev_fin_dma[tp] if t < 2 else vfdma[t - 2])]
                    vS_[t] = pC.add("vector", lambda E, S, t=t, tp=tp: E.scalar_tensor_tensor(s_sb[:, tp, :], acc[:, t, :], invl[:, t:t + 1], xn_ld[:, tp, :], ALU.mult, ALU.add),
                                    waits=wS, inc="dve")
                    psB = psVC if t % 2 == 0 else psT
                    wY = [("dve", vYB.get(t - 2, 0))]
                    if t == 0:
                        wY.append(("dve", vUP[K - 1]))
                    if t == 1:
                        wY.append(("act", vEVT[K - 1]))

                    def fnY(E, S, t=t, sp=sp, psB=psB):
                        last = None
                        for h in range(2):
                            for d in range(8):
                                last = E.matmul(psB[:, h * 512:(h + 1) * 512], xnT_own[:, sp, d, t * 128:(t + 1) * 128],
                                                M_sb[:, d, h * 512:(h + 1) * 512], start=(d == 0), stop=(d == 7))
                        return last
                    vY[t] = pC.add("tensor", fnY, waits=wY, inc="pe")
                    vYB[t] = pC.add("vector", lambda E, S, tp=tp, psB=psB: E.tensor_add(yb_sb[:, tp, :], psB[:], bias_bc[:]),
                                    waits=[("pe", vY[t])], inc="dve")
                    vG[t] = pC.add("scalar", lambda E, S, tp=tp: E.activation(gsil[:, tp, :], yb_sb[:, tp, :], AF.Silu),
                                   waits=[("dve", vYB[t])], inc="act")
                    vMU = pC.add("vector", lambda E, S, tp=tp: E.tensor_mul(yb_sb[:, tp, :], s_sb[:, tp, :], gsil[:, tp, :]),
                                 waits=[("act", vG[t])], inc="dve")
                    vOUT[t] = pC.add("vector", lambda E, S, tp=tp, t4=t4: E.tensor_add(s_sb[:, tp, :], yb_sb[:, tp, :], xstC[:, t4, :]),
                                     waits=[("dve", vMU), ("lxf%d" % t4, vXF[t])], inc="dve")
                    vfdma[t] = pC.add("gpsimd", lambda E, S, s=s, t=t, tp=tp: E.dma_start(out=yown_t[:, s, t, :], in_=s_sb[:, tp, :]),
                                      waits=[("dve", vOUT[t])], inc="sy%d" % tp, by=16)
                    if t + 2 < 8:
                        vxnl2[t + 2] = pC.add("scalar", lambda E, S, s=s, t=t: E.dma_start(out=xn_ld[:, t % 2, :], in_=xn2scr_t[:, s, t + 2, :]),
                                              waits=[("dve", vS_[t])], inc="lxn%d" % (t % 2), by=16)
                    if t + 4 < 8:
                        vXF[t + 4] = pC.add("scalar", lambda E, S, s=s, t=t: E.dma_start(out=xstC[:, t % 4, :], in_=xown_t[:, s, t + 4, :]),
                                            waits=[("dve", vOUT[t])], inc="lxf%d" % (t % 4), by=16)

                fin_pe_bypar[sp] = pC.val("pe")
                prev_psVC_free = vYB[6]
                prev_psT_free = vYB[7]
                prev_fin_dma = {0: vfdma[6], 1: vfdma[7]}
                prev_out4 = {0: vOUT[4], 1: vOUT[5], 2: vOUT[6], 3: vOUT[7]}
                prev_Sfree_bypar = {0: vS_[6], 1: vS_[7]}
                prev_ex_bybank = {0: vEX[K - 2], 1: vEX[K - 1]}
                prev_tr_bypar = {0: vTR[K - 2], 1: vTR[K - 1]}
                prev_evt_last = vEVT[K - 1]

            pC.add("gpsimd", lambda E, S: None, waits=[("sy0", pC.val("sy0")), ("sy1", pC.val("sy1"))])
            dispatch(pC)

    return nc


def _host_consts():
    k = np.arange(D)
    ang = 2.0 * np.pi * (np.outer(k, k) % D).astype(np.float64) / D
    bf = ml_dtypes.bfloat16
    consts = {
        "Fr_h": np.cos(ang).astype(bf),
        "Fi_h": (-np.sin(ang)).astype(bf),
        "Gr_h": (np.cos(ang) / D).astype(bf),
        "Gin_h": (-np.sin(ang) / D).astype(bf),
        "ident_h": np.eye(128, dtype=bf),
    }
    tri = np.where(np.arange(128)[None, :] <= np.arange(128)[:, None], 0.0, NEG_BIG)
    consts["trimask_h"] = tri.astype(bf)
    return consts


def _btable(lane, perm):
    bt = np.zeros((128, NITER), dtype=np.float32)
    for s in range(NSLOT):
        c = chunk_of(lane, s)
        for j in range(LTRIP[s]):
            if perm[j] >= c:
                bt[:, SLOT_OFF[s] + j] = -NEG_BIG
    return bt


def kernel(L0_attn_x, L0_attn_W, L0_ffn_w_freq_real, L0_ffn_w_freq_imag, L0_ffn_bias):
    x = np.ascontiguousarray(np.asarray(L0_attn_x, dtype=np.float32))
    W = np.asarray(L0_attn_W, dtype=np.float32)
    wfr = np.ascontiguousarray(np.asarray(L0_ffn_w_freq_real, dtype=np.float32))
    wfi = np.ascontiguousarray(np.asarray(L0_ffn_w_freq_imag, dtype=np.float32))
    bias = np.ascontiguousarray(np.asarray(L0_ffn_bias, dtype=np.float32))
    WTh = np.ascontiguousarray(W.T)

    consts = _host_consts()
    nc = build_nc()
    in_maps = []
    for core in range(8):
        b, lane = core // 4, core % 4
        xb = x[b]
        perm = perm_of(lane)
        xb_perm = np.ascontiguousarray(
            xb.reshape(16, 1024, D)[perm].reshape(16 * 1024, D))
        xown = np.stack([xb[chunk_of(lane, s) * 1024:(chunk_of(lane, s) + 1) * 1024] for s in range(NSLOT)])
        m = {
            "x_all": xb_perm,
            "x_own": np.ascontiguousarray(xown),
            "WTh": WTh,
            "wfr": wfr,
            "wfi": wfi,
            "bias_h": bias,
            "btab_h": _btable(lane, perm),
        }
        m.update(consts)
        in_maps.append(m)

    res = run_bass_kernel_spmd(nc, in_maps, core_ids=list(range(8)))
    out = np.empty((2, 16384, D), dtype=np.float32)
    for core in range(8):
        b, lane = core // 4, core % 4
        y = res.results[core]["y_own"]
        for s in range(NSLOT):
            c = chunk_of(lane, s)
            out[b, c * 1024:(c + 1) * 1024] = y[s]
    return out
